# version 13
# baseline (speedup 1.0000x reference)
import numpy as np
import ml_dtypes
from contextlib import ExitStack

import concourse.bass as bass
import concourse.mybir as mybir
from concourse.bass_utils import run_bass_kernel_spmd

F32 = mybir.dt.float32
I32 = mybir.dt.int32
BF16 = mybir.dt.bfloat16
ALU = mybir.AluOpType
AF = mybir.ActivationFunctionType
AX = mybir.AxisListType.X
BF = ml_dtypes.bfloat16

NCORE = 8
S = 16384
D = 1024
SH = S // NCORE
HALF = SH // 2
NSUP = 16
QT = 256
NQT = HALF // QT

INVPI = float(1.0 / np.pi)
PI_F32 = float(np.float32(np.pi))
HALFPI = float(np.float32(np.pi / 2))
P1C = 3.140625
P2C = float((np.float32(np.float64(np.pi) - P1C).view(np.uint32) & np.uint32(0xFFFFF000)).view(np.float32))
P3C = float(np.float32(np.float64(np.pi) - P1C - P2C))
H1C = float(np.float32(np.pi / 2))
H2C = float(np.float32(np.float64(np.pi / 2) - H1C))


def build_nc(dbg=frozenset()):
    nc = bass.Bass("TRN2", target_bir_lowering=False)

    x_h = nc.dram_tensor("x_own", [SH, D], F32, kind="ExternalInput").ap()
    wq_h = nc.dram_tensor("wq", [D, D], F32, kind="ExternalInput").ap()
    wk_h = nc.dram_tensor("wk", [D, D], F32, kind="ExternalInput").ap()
    wv_h = nc.dram_tensor("wv", [D, D], F32, kind="ExternalInput").ap()
    bq_h = nc.dram_tensor("bq2", [128, 8], F32, kind="ExternalInput").ap()
    bk_h = nc.dram_tensor("bk2", [128, 8], F32, kind="ExternalInput").ap()
    bv_h = nc.dram_tensor("bvrow", [1, D], F32, kind="ExternalInput").ap()
    pb_h = nc.dram_tensor("pb2", [128, 8], F32, kind="ExternalInput").ap()
    iwl_h = nc.dram_tensor("iwl", [128, 8], F32, kind="ExternalInput").ap()
    g1r_h = nc.dram_tensor("g1r", [4, 1024], F32, kind="ExternalInput").ap()
    g1i_h = nc.dram_tensor("g1i", [4, 1024], F32, kind="ExternalInput").ap()
    g2r_h = nc.dram_tensor("g2r", [4, 256], F32, kind="ExternalInput").ap()
    g2i_h = nc.dram_tensor("g2i", [4, 256], F32, kind="ExternalInput").ap()
    fb_h = nc.dram_tensor("fbrow", [1, D], F32, kind="ExternalInput").ap()
    idn_h = nc.dram_tensor("idn", [128, 128], BF16, kind="ExternalInput").ap()
    idf_h = nc.dram_tensor("idf", [8, 8], F32, kind="ExternalInput").ap()
    eoh_h = nc.dram_tensor("eoh", [128, 16], BF16, kind="ExternalInput").ap()
    ones_h = nc.dram_tensor("onesrow", [1, 128], BF16, kind="ExternalInput").ap()
    y_h = nc.dram_tensor("y_own", [SH, D], F32, kind="ExternalOutput").ap()

    wq_bf = nc.dram_tensor("wq_bf", [D, D], BF16)
    wk_bf = nc.dram_tensor("wk_bf", [D, D], BF16)
    wv_bf = nc.dram_tensor("wv_bf", [D, D], BF16)
    agin_ck = nc.dram_tensor("agin_ck", [D, SH], BF16)
    agin_sk = nc.dram_tensor("agin_sk", [D, SH], BF16)
    agin_v = nc.dram_tensor("agin_v", [SH, D], BF16)
    ck_full = nc.dram_tensor("ck_full", [D * NCORE, SH], BF16, addr_space="Shared")
    sk_full = nc.dram_tensor("sk_full", [D * NCORE, SH], BF16, addr_space="Shared")
    v_full = nc.dram_tensor("v_full", [S, D], BF16, addr_space="Shared")

    es = ExitStack()
    with es:
        sb = lambda n, sh, dt: es.enter_context(nc.sbuf_tensor(n, sh, dt))
        pp = lambda n, sh, dt: es.enter_context(nc.psum_tensor(n, sh, dt))

        wqT = sb("wqT", [128, 8 * 1024], BF16)
        wrT = sb("wrT", [128, 4 * 512], BF16)
        wiT = sb("wiT", [128, 4 * 512], BF16)
        nwiT = sb("nwiT", [128, 4 * 512], BF16)
        iwl = sb("iwl_t", [128, 8], F32)
        btq = sb("btq", [128, 8], F32)
        btk = sb("btk", [128, 8], F32)
        bq2 = sb("bq2_t", [128, 8], F32)
        bk2 = sb("bk2_t", [128, 8], F32)
        pb2 = sb("pb2_t", [128, 8], F32)
        bvr = sb("bvr", [1, D], BF16)
        fbr = sb("fbr", [1, D], BF16)
        idn = sb("idn_t", [128, 128], BF16)
        idf = sb("idf_t", [8, 8], F32)
        eoh = sb("eoh_t", [128, 16], BF16)
        onesr = sb("ones_t", [1, 128], BF16)

        xt = [sb(f"xt{i}", [128, D], F32) for i in range(2)]
        iwlp = sb("iwlp", [128, 8], F32)
        btqp = sb("btqp", [128, 8], F32)
        btkp = sb("btkp", [128, 8], F32)
        btqp2 = sb("btqp2", [128, 8], F32)
        btkp2 = sb("btkp2", [128, 8], F32)
        bnb8 = sb("bnb8", [128, 96], F32)
        bna8 = sb("bna8", [128, 16], F32)
        v8 = sb("v8", [128, 8], F32)
        isd8 = sb("isd8", [128, 8], F32)
        ST = {}

        def alloc_staging(stack, tag):
            sbx = lambda n, sh, dt: stack.enter_context(nc.sbuf_tensor(n + tag, sh, dt))
            ST["lnx"] = sbx("lnx", [128, 2 * D], BF16)
            ST["xg"] = sbx("xg", [128, 4 * 1024], F32)
            ST["lnxF"] = sbx("lnxF", [128, 8 * 512], BF16)
            ST["kA"] = sbx("kA", [128, 1024], I32)
            ST["kS"] = sbx("kS", [128, 2048], I32)
            ST["kC"] = sbx("kC", [128, 2048], I32)
            ST["rcT"] = sbx("rcT", [128, 1024], F32)
            ST["sgT"] = sbx("sgT", [128, 1024], BF16)
            ST["snT"] = sbx("snT", [128, 1024], BF16)
            ST["tht"] = sbx("tht", [128, 2048], F32)
            ST["cosb"] = sbx("cosb", [128, 1024], BF16)
            ST["sinb"] = sbx("sinb", [128, 1024], BF16)

        PS = {}

        def alloc_psum_build(stack, tag):
            ppx = lambda n, sh, dt: stack.enter_context(nc.psum_tensor(n + tag, sh, dt))
            PS["sc"] = [ppx(f"sc{i}", [128, 512], F32) for i in range(2)]
            PS["pT"] = [ppx(f"pT{i}", [128, 1024], BF16) for i in range(2)]

        def alloc_psum_attn(stack, tag):
            ppx = lambda n, sh, dt: stack.enter_context(nc.psum_tensor(n + tag, sh, dt))
            PS["pv"] = [ppx(f"pv{i}", [128, 512], F32) for i in range(4)]
            PS["sc"] = [ppx(f"sc{i}", [128, 512], F32) for i in range(3)]
            PS["sm"] = ppx("sums", [128, 512], F32)

        sem = {}
        for s_ in ("ld", "gd", "mm", "vv", "ac", "so", "cc", "xa", "xb", "tw", "sl", "s2", "s3", "s4"):
            sem[s_] = es.enter_context(nc.semaphore(s_))
        cnt = dict.fromkeys(sem, 0)

        block = es.enter_context(nc.Block())

        T = nc.tensor
        V = nc.vector
        A = nc.scalar
        SY = nc.sync
        G = nc.gpsimd

        def inc(instr, s_, amt=1):
            instr.then_inc(sem[s_], amt)
            cnt[s_] += amt
            return cnt[s_]

        def wait(eng, s_):
            eng.wait_ge(sem[s_], cnt[s_])

        def vop(instr):
            inc(instr, "vv")
            V.wait_ge(sem["vv"], cnt["vv"])

        def dma(eng, s_, out, in_, transpose=False):
            eng.wait_ge(sem[s_], cnt[s_])
            ins = (eng.dma_start_transpose(out=out, in_=in_) if transpose
                   else eng.dma_start(out=out, in_=in_))
            inc(ins, s_, 16)
            return cnt[s_]

        def emit_trig(th, cos_o, sin_o, pp_=0):
            kA = ST["kA"]; rcT = ST["rcT"]; sgT = ST["sgT"]; snT = ST["snT"]
            for mode in ("sin", "cos"):
                kin = (ST["kS"] if mode == "sin" else ST["kC"])[:, pp_ * 1024:(pp_ + 1) * 1024]
                vop(V.scalar_tensor_tensor(out=rcT[:], in0=kin[:], scalar=-PI_F32, in1=th, op0=ALU.mult, op1=ALU.add))
                hi = HALFPI if mode == "sin" else 0.0
                lo = -HALFPI if mode == "sin" else -PI_F32
                vop(V.scalar_tensor_tensor(out=kA[:], in0=rcT[:], scalar=hi, in1=kin[:], op0=ALU.is_gt, op1=ALU.add))
                vop(V.scalar_tensor_tensor(out=kA[:], in0=rcT[:], scalar=lo, in1=kA[:], op0=ALU.is_lt, op1=ALU.subtract))
                vop(V.scalar_tensor_tensor(out=rcT[:], in0=kA[:], scalar=P1C, in1=th, op0=ALU.mult, op1=ALU.add))
                vop(V.scalar_tensor_tensor(out=rcT[:], in0=kA[:], scalar=P2C, in1=rcT[:], op0=ALU.mult, op1=ALU.add))
                vop(V.scalar_tensor_tensor(out=rcT[:], in0=kA[:], scalar=P3C, in1=rcT[:], op0=ALU.mult, op1=ALU.add))
                if mode == "cos":
                    vop(V.tensor_scalar(out=rcT[:], in0=rcT[:], scalar1=H1C, scalar2=H2C, op0=ALU.add, op1=ALU.add))
                vop(V.tensor_scalar(out=kA[:], in0=kA[:], scalar1=1, scalar2=None, op0=ALU.bitwise_and))
                vop(V.tensor_scalar(out=sgT[:], in0=kA[:], scalar1=-2.0, scalar2=None, op0=ALU.mult))
                wait(A, "vv")
                acv = inc(A.activation(snT[:], rcT[:], AF.Sin), "ac")
                V.wait_ge(sem["ac"], acv)
                out = sin_o if mode == "sin" else cos_o
                vop(V.scalar_tensor_tensor(out=out, in0=sgT[:], scalar=1.0, in1=snT[:], op0=ALU.add, op1=ALU.mult))

        for dst, src in ((iwl, iwl_h), (bq2, bq_h), (bk2, bk_h), (pb2, pb_h),
                         (idn, idn_h), (idf, idf_h), (eoh, eoh_h), (onesr, ones_h)):
            dma(SY, "ld", dst[:], src[:, :])
        for wsrc, wdst in ((wq_h, wq_bf), (wk_h, wk_bf), (wv_h, wv_bf)):
            dma(G, "gd", wdst.ap()[:, :], wsrc[:, :])
        dma(G, "gd", bvr[:], bv_h[:, :])
        dma(G, "gd", fbr[:], fb_h[:, :])

        wait(V, "ld")
        vop(V.tensor_tensor(out=btq[:], in0=bq2[:], in1=iwl[:], op=ALU.mult))
        vop(V.tensor_tensor(out=btq[:], in0=btq[:], in1=pb2[:], op=ALU.add))
        vop(V.tensor_tensor(out=btk[:], in0=bk2[:], in1=iwl[:], op=ALU.mult))
        vop(V.tensor_tensor(out=btk[:], in0=btk[:], in1=pb2[:], op=ALU.add))
        vop(V.tensor_scalar(out=iwlp[:], in0=iwl[:], scalar1=INVPI, scalar2=None, op0=ALU.mult))
        vop(V.tensor_scalar(out=btqp[:], in0=btq[:], scalar1=INVPI, scalar2=None, op0=ALU.mult))
        vop(V.tensor_scalar(out=btkp[:], in0=btk[:], scalar1=INVPI, scalar2=None, op0=ALU.mult))
        vop(V.tensor_scalar(out=btqp2[:], in0=btqp[:], scalar1=1.0, scalar2=0.5, op0=ALU.mult, op1=ALU.add))
        vop(V.tensor_scalar(out=btkp2[:], in0=btkp[:], scalar1=1.0, scalar2=0.5, op0=ALU.mult, op1=ALU.add))
        bt_mark = cnt["vv"]

        cosq = sb("cosq", [128, 8 * 1024], BF16)
        sinq = sb("sinq", [128, 8 * 1024], BF16)
        es1 = ExitStack()
        sb1 = lambda n, sh, dt: es1.enter_context(nc.sbuf_tensor(n, sh, dt))
        alloc_staging(es1, "_p1")
        alloc_psum_build(es1, "_p1")
        pt_ac = [0, 0]
        xg_marks = [0, 0, 0, 0]
        xld_marks = [0, 0, 0, 0]
        trig_marks = [0, 0]
        g1r_t = sb1("g1r_t", [4, 1024], BF16)
        g1i_t = sb1("g1i_t", [4, 1024], BF16)
        g2r_t = sb1("g2r_t", [4, 256], BF16)
        g2i_t = sb1("g2i_t", [4, 256], BF16)
        outg = sb1("outg", [128, 1024], BF16)
        vevac = sb1("vevac", [128, D], BF16)
        wkT = sb1("wkT", [128, 8 * 1024], BF16)
        wvT = sb1("wvT", [128, 8 * 1024], BF16)
        for dst, src in ((g1r_t, g1r_h), (g1i_t, g1i_h), (g2r_t, g2r_h), (g2i_t, g2i_h)):
            dma(G, "gd", dst[:], src[:, :])
        wait(T, "gd")
        wait(T, "ld")

        for g1t, g2t, wdstT in (() if "p0f" in dbg else ((g1r_t, g2r_t, wrT), (g1i_t, g2i_t, wiT))):
            for C in range(4):
                for gl in range(4):
                    g = 4 * C + gl
                    stat = g2t[:, g::16]
                    sc = PS["sc"]
                    pT = PS["pT"][0]
                    T.matmul(sc[0][0:16, 0:512], stat, g1t[:, 0:512], start=True, stop=True)
                    mmv = inc(T.matmul(sc[1][0:16, 0:512], stat, g1t[:, 512:1024], start=True, stop=True), "mm")
                    V.wait_ge(sem["mm"], mmv)
                    vop(V.tensor_copy(outg[0:16, 0:512], sc[0][0:16, 0:512]))
                    vop(V.tensor_copy(outg[0:16, 512:1024], sc[1][0:16, 0:512]))
                    wait(T, "vv")
                    if "ffnw" not in dbg:
                        for b in range(32):
                            if gl < 3:
                                dst = pT[gl * 32:(gl + 1) * 32, b * 16:(b + 1) * 16]
                            else:
                                dst = pT[64:96, 512 + b * 16: 512 + (b + 1) * 16]
                            ins_ = T.matmul(dst, outg[0:16, b * 32:(b + 1) * 32],
                                            idn[0:16, 0:16],
                                            start=True, stop=True, is_transpose=True)
                        inc(ins_, "mm")
                    else:
                        inc(T.nop(), "mm")
                wait(V, "mm")
                vop(V.tensor_copy(wdstT[0:96, C * 512:(C + 1) * 512], PS["pT"][0][0:96, 0:512]))
                vop(V.tensor_copy(wdstT[96:128, C * 512:(C + 1) * 512], PS["pT"][0][64:96, 512:1024]))
                wait(T, "vv")
        vop(V.tensor_scalar(out=nwiT[:], in0=wiT[:], scalar1=-1.0, scalar2=None, op0=ALU.mult))


        def emit_xloads(tok0):
            xg = ST["xg"]
            for tt_ in range(4):
                psem = "xa" if tt_ % 2 == 0 else "xb"
                SY.wait_ge(sem["vv"], xg_marks[tt_])
                dma(SY, psem, xg[:, tt_ * 1024:(tt_ + 1) * 1024],
                    x_h[tok0 + tt_ * 128: tok0 + (tt_ + 1) * 128, :])
                xld_marks[tt_] = cnt[psem]

        def emit_group_build(tok0, proj_passes, write_v, next_tok0=None):
            lnx = ST["lnx"]; lnxF = ST["lnxF"]; tht = ST["tht"]; cosb = ST["cosb"]; sinb = ST["sinb"]
            xg = ST["xg"]
            for tt_ in range(4):
                psem = "xa" if tt_ % 2 == 0 else "xb"
                V.wait_ge(sem[psem], xld_marks[tt_])
                vop(V.bn_stats(bnb8[:, tt_ * 12: tt_ * 12 + 6], xg[:, tt_ * 1024: tt_ * 1024 + 512]))
                vop(V.bn_stats(bnb8[:, tt_ * 12 + 6: tt_ * 12 + 12], xg[:, tt_ * 1024 + 512: (tt_ + 1) * 1024]))
                vop(V.bn_aggr(bna8[:, tt_ * 2: tt_ * 2 + 2], bnb8[:, tt_ * 12: tt_ * 12 + 12]))
            vop(V.tensor_scalar(out=v8[:, 0:4], in0=bna8[:, 1:8:2], scalar1=1.0, scalar2=1e-5,
                                op0=ALU.mult, op1=ALU.add))
            vop(V.reciprocal(v8[:, 0:4], v8[:, 0:4]))
            wait(A, "vv")
            acv = inc(A.activation(isd8[:, 0:4], v8[:, 0:4], AF.Sqrt), "ac")
            V.wait_ge(sem["ac"], acv)
            lnx_mm = [0, 0]
            for tt_ in range(4):
                bk = tt_ % 2
                lx = lnx[:, bk * D:(bk + 1) * D]
                V.wait_ge(sem["mm"], lnx_mm[bk])
                vop(V.tensor_scalar(out=lx, in0=xg[:, tt_ * 1024:(tt_ + 1) * 1024],
                                    scalar1=bna8[:, tt_ * 2: tt_ * 2 + 1], scalar2=isd8[:, tt_: tt_ + 1],
                                    op0=ALU.subtract, op1=ALU.mult))
                xg_marks[tt_] = cnt["vv"]
                pTb = PS["pT"][bk]
                T.wait_ge(sem["ac"], pt_ac[bk])
                wait(T, "vv")
                for dchunk in range(8):
                    ins_ = T.matmul(pTb[:, dchunk * 128:(dchunk + 1) * 128],
                                    lx[:, dchunk * 128:(dchunk + 1) * 128], idn[:],
                                    start=True, stop=True, is_transpose=True)
                mmv = inc(ins_, "mm")
                lnx_mm[bk] = mmv
                A.wait_ge(sem["mm"], mmv)
                for dchunk in range(8):
                    acv = inc(A.activation(lnxF[:, dchunk * 512 + tt_ * 128: dchunk * 512 + (tt_ + 1) * 128],
                                           pTb[:, dchunk * 128:(dchunk + 1) * 128], AF.Identity), "ac")
                pt_ac[bk] = cnt["ac"]
            if next_tok0 is not None:
                emit_xloads(next_tok0)
            wait(T, "ac")

            wait(T, "ld")
            for (wT, bt, write_trig, cosq_dst, sinq_dst, qcol0) in proj_passes:
                sc = PS["sc"]
                btp, btp2 = (btqp, btqp2) if bt is btq else (btkp, btkp2)
                A.wait_ge(sem["vv"], bt_mark)
                for ec in range(8):
                    ps = sc[ec % 2]
                    for dc in range(8):
                        ins_ = T.matmul(ps[:, 0:512], wT[:, dc * 1024 + ec * 128: dc * 1024 + (ec + 1) * 128],
                                        lnxF[:, dc * 512:(dc + 1) * 512],
                                        start=(dc == 0), stop=(dc == 7))
                    mmv = inc(ins_, "mm")
                    A.wait_ge(sem["mm"], mmv)
                    pp_ = (ec // 2) % 2
                    if ec % 2 == 0:
                        A.wait_ge(sem["vv"], trig_marks[pp_])
                    half = pp_ * 1024 + (ec % 2) * 512
                    inc(A.activation(ST["kS"][:, half:half + 512], ps[:, 0:512], AF.Identity,
                                     bias=btp[:, ec:ec + 1], scale=iwlp[:, ec:ec + 1]), "ac")
                    inc(A.activation(ST["kC"][:, half:half + 512], ps[:, 0:512], AF.Identity,
                                     bias=btp2[:, ec:ec + 1], scale=iwlp[:, ec:ec + 1]), "ac")
                    acv = inc(A.activation(tht[:, half:half + 512], ps[:, 0:512],
                                           AF.Identity,
                                           bias=bt[:, ec:ec + 1], scale=iwl[:, ec:ec + 1]), "ac")
                    V.wait_ge(sem["ac"], acv)
                    T.wait_ge(sem["ac"], acv)
                    if ec % 2 == 0:
                        continue
                    tslice = tht[:, pp_ * 1024:(pp_ + 1) * 1024]
                    if cosq_dst is None:
                        wait(V, "tw")
                        emit_trig(tslice, cosb[:], sinb[:], pp_=pp_)
                        dcos, dsin, colbase = write_trig
                        wait(SY, "vv")
                        dma(SY, "tw", dcos.ap()[(ec - 1) * 128:(ec + 1) * 128, colbase:colbase + 512]
                            .rearrange("(c p) q -> p c q", p=128),
                            cosb[:, :].rearrange("p (c q) -> p c q", c=2))
                        dma(SY, "tw", dsin.ap()[(ec - 1) * 128:(ec + 1) * 128, colbase:colbase + 512]
                            .rearrange("(c p) q -> p c q", p=128),
                            sinb[:, :].rearrange("p (c q) -> p c q", c=2))
                    else:
                        cq3 = cosq_dst[:, :].rearrange("p (c q) -> p c q", c=8)
                        sq3 = sinq_dst[:, :].rearrange("p (c q) -> p c q", c=8)
                        emit_trig(tslice,
                                  cq3[:, ec - 1:ec + 1, qcol0:qcol0 + 512],
                                  sq3[:, ec - 1:ec + 1, qcol0:qcol0 + 512], pp_=pp_)
                    trig_marks[pp_] = cnt["vv"]

            if write_v is not None:
                sc = PS["sc"]
                for ts_ in range(4):
                    for eh in range(2):
                        ps = sc[eh]
                        for dc in range(8):
                            T.matmul(ps[:, 0:512],
                                     lnxF[:, dc * 512 + ts_ * 128: dc * 512 + (ts_ + 1) * 128],
                                     wvT[:, dc * 1024 + eh * 512: dc * 1024 + (eh + 1) * 512],
                                     start=(dc == 0), stop=False)
                        ins_ = T.matmul(ps[:, 0:512], onesr[0:1, 0:128],
                                        bvr[0:1, eh * 512:(eh + 1) * 512], start=False, stop=True)
                    mmv = inc(ins_, "mm")
                    A.wait_ge(sem["mm"], mmv)
                    wait(A, "tw")
                    inc(A.activation(vevac[:, 0:512], sc[0][:, 0:512], AF.Identity), "ac")
                    acv = inc(A.activation(vevac[:, 512:1024], sc[1][:, 0:512], AF.Identity), "ac")
                    T.wait_ge(sem["ac"], acv)
                    SY.wait_ge(sem["ac"], acv)
                    dma(SY, "tw", agin_v.ap()[write_v + ts_ * 128: write_v + (ts_ + 1) * 128, :],
                        vevac[:])

        if "p1" not in dbg:
            emit_xloads(0)
        wait(SY, "gd")
        for wdst, wsrc in (() if "wtr" in dbg else ((wqT, wq_bf), (wkT, wk_bf), (wvT, wv_bf))):
            for dchunk in range(8):
                dma(SY, "ld", wdst[:, dchunk * 1024:(dchunk + 1) * 1024],
                    wsrc.ap()[:, dchunk * 128:(dchunk + 1) * 128], transpose=True)
        if "p1" not in dbg:
            for grp in range(4):
                passes = [(wkT, btk, (agin_ck, agin_sk, grp * 512), None, None, 0)]
                if grp < 2:
                    passes.append((wqT, btq, None, cosq, sinq, grp * 512))
                emit_group_build(grp * 512, passes, write_v=grp * 512,
                                 next_tok0=(grp + 1) * 512 if grp < 3 else None)
        wait(G, "tw")
        if "ag" not in dbg:
            for ain, aout in ((agin_ck, ck_full), (agin_sk, sk_full), (agin_v, v_full)):
                inc(G.collective_compute(
                    "AllGather", ALU.bypass, replica_groups=[list(range(NCORE))],
                    ins=[ain.ap().opt()], outs=[aout.ap().opt()]), "cc")
        es1.close()

        oacc = sb("oacc", [128, 8 * 1024], F32)
        pts = sb("pts", [128, 8 * 256], BF16)
        sums_sb = sb("sums_sb", [128, 256], F32)
        rinv = sb("rinv", [128, 8], F32)

        for h in range(2):
            hq0 = h * HALF
            wait(V, "mm")
            wait(V, "tw")
            es_a = ExitStack()
            alloc_staging(es_a, f"_h{h}")
            alloc_psum_build(es_a, f"_a{h}")
            pt_ac = [0, 0]
            xg_marks = [cnt["vv"]] * 4
            xld_marks = [0, 0, 0, 0]
            trig_marks = [cnt["vv"], cnt["vv"]]
            if "p2" not in dbg and h == 1:
                wait(SY, "tw")
                wait(SY, "vv")
                wait(SY, "ac")
                wait(SY, "mm")
                emit_xloads(hq0)
                for grp in range(2):
                    emit_group_build(hq0 + grp * 512,
                                     [(wqT, btq, None, cosq, sinq, grp * 512)],
                                     write_v=None,
                                     next_tok0=hq0 + 512 if grp == 0 else None)
            es_a.close()

            wait(SY, "cc")
            wait(T, "vv")

            es_b = ExitStack()
            alloc_psum_attn(es_b, f"_b{h}")
            pv = PS["pv"]; sm = PS["sm"]
            sbb = lambda n, sh, dt: es_b.enter_context(nc.sbuf_tensor(n + f"_h{h}", sh, dt))
            ckb = sbb("ckb", [128, 2 * 8 * 1024], BF16)
            skb = sbb("skb", [128, 2 * 8 * 1024], BF16)
            vbb = sbb("vbb", [128, 2 * 8 * 1024], BF16)
            SLOT = 8 * 1024

            sup_marks = []
            cons_marks = []

            def load_super(s_g):
                if s_g >= 2:
                    SY.wait_ge(sem["mm"], cons_marks[s_g - 2])
                r = s_g // 2
                koff = (s_g % 2) * 1024
                so_ = (s_g % 2) * SLOT
                for hl in range(2):
                    dma(SY, "sl",
                        ckb[:, so_:so_ + SLOT].rearrange("p (c k) -> p c k", c=8)[:, :, hl * 512:(hl + 1) * 512],
                        ck_full.ap()[r * 1024:(r + 1) * 1024, koff + hl * 512: koff + (hl + 1) * 512]
                            .rearrange("(c p) k -> p c k", p=128))
                    dma(SY, "s2",
                        skb[:, so_:so_ + SLOT].rearrange("p (c k) -> p c k", c=8)[:, :, hl * 512:(hl + 1) * 512],
                        sk_full.ap()[r * 1024:(r + 1) * 1024, koff + hl * 512: koff + (hl + 1) * 512]
                            .rearrange("(c p) k -> p c k", p=128))
                    dma(SY, "s3",
                        vbb[:, so_:so_ + SLOT].rearrange("p (c e) -> p c e", c=8)[:, hl * 4:(hl + 1) * 4, :],
                        v_full.ap()[s_g * 1024 + hl * 512: s_g * 1024 + (hl + 1) * 512, :]
                            .rearrange("(c p) e -> p c e", p=128))
                sup_marks.append((cnt["sl"], cnt["s2"], cnt["s3"]))

            wait(SY, "mm")
            wait(SY, "vv")
            wait(SY, "ac")
            load_super(0)
            load_super(1)
            exp_acv = []
            gchunk = 0
            evac_vv = [0, 0, 0, 0]
            for sp in range(NSUP if "attn" not in dbg else 0):
                for qt in range(NQT):
                    for j in range(9):
                        if j < 8:
                            mk = sup_marks[sp]
                            T.wait_ge(sem["sl"], mk[0])
                            T.wait_ge(sem["s2"], mk[1])
                            T.wait_ge(sem["s3"], mk[2])
                            ps = PS["sc"][gchunk % 3]
                            if gchunk >= 3:
                                T.wait_ge(sem["ac"], exp_acv[gchunk - 3])
                            first = True
                            so_ = (sp % 2) * SLOT
                            for opa, opb in ((ckb, cosq), (skb, sinq)):
                                for dc in range(8):
                                    ins_ = T.matmul(ps[:, 0:256],
                                                    opa[:, so_ + dc * 1024 + j * 128: so_ + dc * 1024 + (j + 1) * 128],
                                                    opb[:, dc * 1024 + qt * 256: dc * 1024 + qt * 256 + 256],
                                                    start=first, stop=(opa is skb and dc == 7))
                                    first = False
                            mmv = inc(ins_, "mm")
                            A.wait_ge(sem["mm"], mmv)
                            acv = inc(A.activation(pts[:, j * 256:(j + 1) * 256], ps[:, 0:256],
                                                   AF.Exp, scale=1.0 / 32.0), "ac")
                            exp_acv.append(acv)
                            gchunk += 1
                        if j > 0:
                            jj = j - 1
                            T.wait_ge(sem["ac"], exp_acv[gchunk - (2 if j < 8 else 1)])
                            for qs in range(2):
                                for eh in range(2):
                                    if jj == 0:
                                        T.wait_ge(sem["vv"], evac_vv[qs * 2 + eh])
                                    T.matmul(pv[qs * 2 + eh][:, 0:512],
                                             pts[:, jj * 256 + qs * 128: jj * 256 + qs * 128 + 128],
                                             vbb[:, (sp % 2) * SLOT + jj * 1024 + eh * 512:
                                                 (sp % 2) * SLOT + jj * 1024 + (eh + 1) * 512],
                                             start=(jj == 0), stop=(jj == 7))
                            ins_ = T.matmul(sm[0:4, 0:256], eoh[:, qt * 4:(qt + 1) * 4],
                                            pts[:, jj * 256:(jj + 1) * 256],
                                            start=(sp == 0 and qt == 0 and jj == 0),
                                            stop=(sp == NSUP - 1 and qt == NQT - 1 and jj == 7))
                    mmv = inc(ins_, "mm")
                    if qt == NQT - 1:
                        cons_marks.append(mmv)
                        if sp + 2 < NSUP:
                            load_super(sp + 2)
                    V.wait_ge(sem["mm"], mmv)
                    for qs in range(2):
                        for eh in range(2):
                            dst = oacc[:, (qt * 2 + qs) * 1024 + eh * 512:
                                       (qt * 2 + qs) * 1024 + (eh + 1) * 512]
                            if sp == 0:
                                vop(V.tensor_copy(dst, pv[qs * 2 + eh][:, 0:512]))
                            else:
                                vop(V.tensor_tensor(out=dst, in0=dst, in1=pv[qs * 2 + eh][:, 0:512], op=ALU.add))
                            evac_vv[qs * 2 + eh] = cnt["vv"]

            wait(V, "mm")
            vop(V.tensor_copy(sums_sb[0:4, :], sm[0:4, 0:256]))
            wait(T, "vv")
            ins_ = T.nop()
            for qs_ in range(2 if "sumt" not in dbg else 0):
                ins_ = T.matmul(pv[0][0:128, qs_ * 4:(qs_ + 1) * 4],
                                sums_sb[0:4, qs_ * 128:(qs_ + 1) * 128],
                                idf[0:4, 0:4], start=(qs_ == 0), stop=(qs_ == 1), is_transpose=True)
            mmv = inc(ins_, "mm")
            V.wait_ge(sem["mm"], mmv)
            vop(V.reciprocal(rinv[:], pv[0][0:128, 0:8]))
            wait(T, "vv")
            es_b.close()

            es_c = ExitStack()
            alloc_psum_build(es_c, f"_c{h}")
            pt_ac = [0, 0]
            x1v = [cnt["vv"], cnt["vv"]]
            sbc = lambda n, sh, dt: es_c.enter_context(nc.sbuf_tensor(n + f"_h{h}", sh, dt))
            x1a = sbc("x1a", [128, 8 * 1024], F32)
            lnx2 = sbc("lnx2", [128, D], BF16)
            lnx2F = sbc("lnx2F", [128, 8 * 128], BF16)
            outt = [sbc("outt0", [128, D], F32), sbc("outt1", [128, D], F32)]

            for sub in range(8):
                buf = xt[sub % 2]
                psem = "xa" if sub % 2 == 0 else "xb"
                SY.wait_ge(sem["vv"], x1v[sub % 2])
                dma(SY, psem, buf[:], x_h[hq0 + sub * 128: hq0 + (sub + 1) * 128, :])
                wait(V, psem)
                rcol = (sub % 2) * 4 + (sub // 2)
                vop(V.scalar_tensor_tensor(out=x1a[:, sub * 1024:(sub + 1) * 1024],
                                           in0=oacc[:, sub * 1024:(sub + 1) * 1024],
                                           scalar=rinv[:, rcol:rcol + 1], in1=buf[:],
                                           op0=ALU.mult, op1=ALU.add))
                x1v[sub % 2] = cnt["vv"]
                vop(V.bn_stats(bnb8[:, sub * 12: sub * 12 + 6], x1a[:, sub * 1024: sub * 1024 + 512]))
                vop(V.bn_stats(bnb8[:, sub * 12 + 6: sub * 12 + 12], x1a[:, sub * 1024 + 512: (sub + 1) * 1024]))
                vop(V.bn_aggr(bna8[:, sub * 2: sub * 2 + 2], bnb8[:, sub * 12: sub * 12 + 12]))
            vop(V.tensor_scalar(out=v8[:], in0=bna8[:, 1::2], scalar1=1.0, scalar2=1e-5,
                                op0=ALU.mult, op1=ALU.add))
            vop(V.reciprocal(v8[:], v8[:]))
            wait(A, "vv")
            acv = inc(A.activation(isd8[:], v8[:], AF.Sqrt), "ac")
            V.wait_ge(sem["ac"], acv)

            for sub in range(8):
                rows = hq0 + sub * 128
                out_b = outt[sub % 2]
                osem = "so" if sub % 2 == 0 else "s4"
                x1s = x1a[:, sub * 1024:(sub + 1) * 1024]
                ln2ap = lnx2[:, :].rearrange("p (ih gh gl d) -> p ih gh gl d", ih=2, gh=4, gl=4, d=32)
                xin = x1s.rearrange("p (ih d gh gl) -> p ih gh gl d", ih=2, d=32, gh=4, gl=4)
                wait(V, "mm")
                vop(V.tensor_scalar(out=ln2ap, in0=xin, scalar1=bna8[:, sub * 2: sub * 2 + 1],
                                    scalar2=isd8[:, sub: sub + 1], op0=ALU.subtract, op1=ALU.mult))
                wait(T, "vv")
                T.wait_ge(sem["ac"], max(pt_ac))
                for ih in range(2):
                    pTb = PS["pT"][ih]
                    for C in range(4):
                        ins_ = T.matmul(pTb[:, C * 128:(C + 1) * 128],
                                        lnx2[:, (ih * 4 + C) * 128:(ih * 4 + C + 1) * 128],
                                        idn[:], start=True, stop=True, is_transpose=True)
                mmv = inc(ins_, "mm")
                A.wait_ge(sem["mm"], mmv)
                for cc_ in range(8):
                    acv = inc(A.activation(lnx2F[:, cc_ * 128:(cc_ + 1) * 128],
                                           PS["pT"][cc_ // 4][:, (cc_ % 4) * 128:(cc_ % 4 + 1) * 128],
                                           AF.Identity), "ac")
                pt_ac[0] = pt_ac[1] = cnt["ac"]
                wait(T, "ac")
                sc = PS["sc"]
                for C in range(4):
                    T.matmul(sc[0][:, 0:512], lnx2F[:, C * 128:(C + 1) * 128],
                             wrT[:, C * 512:(C + 1) * 512], start=(C == 0), stop=False)
                for C in range(4):
                    T.matmul(sc[0][:, 0:512], lnx2F[:, (4 + C) * 128:(4 + C + 1) * 128],
                             nwiT[:, C * 512:(C + 1) * 512], start=False, stop=False)
                T.matmul(sc[0][:, 0:512], onesr[0:1, 0:128], fbr[0:1, 0:512],
                         start=False, stop=True)
                for C in range(4):
                    T.matmul(sc[1][:, 0:512], lnx2F[:, C * 128:(C + 1) * 128],
                             wiT[:, C * 512:(C + 1) * 512], start=(C == 0), stop=False)
                for C in range(4):
                    T.matmul(sc[1][:, 0:512], lnx2F[:, (4 + C) * 128:(4 + C + 1) * 128],
                             wrT[:, C * 512:(C + 1) * 512], start=False, stop=False)
                mmv = inc(T.matmul(sc[1][:, 0:512], onesr[0:1, 0:128], fbr[0:1, 512:1024],
                                   start=False, stop=True), "mm")
                V.wait_ge(sem["mm"], mmv)
                wait(V, osem)
                vop(V.tensor_tensor(out=out_b[:, 0:512], in0=x1s[:, 0:512], in1=sc[0][:, 0:512], op=ALU.add))
                vop(V.tensor_tensor(out=out_b[:, 512:1024], in0=x1s[:, 512:1024], in1=sc[1][:, 0:512], op=ALU.add))
                wait(T, "vv")
                wait(SY, "vv")
                dma(SY, osem, y_h[rows:rows + 128, :], out_b[:])
            wait(V, "so")
            wait(V, "s4")
            wait(T, "so")
            wait(T, "s4")
            es_c.close()

        wait(SY, "so")
        wait(SY, "s4")
    return nc


_NC_CACHE = {}


def kernel(L0_attn_x, L0_attn_Wq, L0_attn_bq, L0_attn_Wk, L0_attn_bk, L0_attn_Wv,
           L0_attn_bv, L0_attn_phase_bias, L0_ffn_G1_R, L0_ffn_G2_R, L0_ffn_G1_I,
           L0_ffn_G2_I, L0_ffn_bias_real, L0_ffn_bias_imag):
    x = np.ascontiguousarray(np.asarray(L0_attn_x, dtype=np.float32).reshape(S, D))
    wq = np.ascontiguousarray(np.asarray(L0_attn_Wq, dtype=np.float32))
    wk = np.ascontiguousarray(np.asarray(L0_attn_Wk, dtype=np.float32))
    wv = np.ascontiguousarray(np.asarray(L0_attn_Wv, dtype=np.float32))
    bq2 = np.ascontiguousarray(np.asarray(L0_attn_bq, dtype=np.float32).reshape(8, 128).T)
    bk2 = np.ascontiguousarray(np.asarray(L0_attn_bk, dtype=np.float32).reshape(8, 128).T)
    pb2 = np.ascontiguousarray(np.asarray(L0_attn_phase_bias, dtype=np.float32).reshape(8, 128).T)
    bvrow = np.ascontiguousarray(np.asarray(L0_attn_bv, dtype=np.float32).reshape(1, D))
    wl = np.arange(1, D + 1, dtype=np.float64) * (2.0 * np.pi / D)
    iwl2 = np.ascontiguousarray((1.0 / (wl + 1e-8)).astype(np.float32).reshape(8, 128).T)
    g1r = np.ascontiguousarray(np.asarray(L0_ffn_G1_R, dtype=np.float32).reshape(32, 4, 32)
                               .transpose(1, 0, 2).reshape(4, 1024))
    g1i = np.ascontiguousarray(np.asarray(L0_ffn_G1_I, dtype=np.float32).reshape(32, 4, 32)
                               .transpose(1, 0, 2).reshape(4, 1024))
    g2r = np.ascontiguousarray(np.asarray(L0_ffn_G2_R, dtype=np.float32).reshape(4, 256))
    g2i = np.ascontiguousarray(np.asarray(L0_ffn_G2_I, dtype=np.float32).reshape(4, 256))
    fbrow = np.ascontiguousarray(
        np.concatenate([np.asarray(L0_ffn_bias_real, dtype=np.float32),
                        np.asarray(L0_ffn_bias_imag, dtype=np.float32)]).reshape(1, D))
    idn = np.eye(128, dtype=np.float32).astype(BF)
    idf = np.eye(8, dtype=np.float32)
    eoh = np.zeros((128, 16), dtype=np.float32)
    for t in range(4):
        eoh[:, 4 * t + t] = 1.0
    eoh = eoh.astype(BF)
    onesrow = np.ones((1, 128), dtype=np.float32).astype(BF)

    if "nc" not in _NC_CACHE:
        _NC_CACHE["nc"] = build_nc()
    nc = _NC_CACHE["nc"]

    shared = dict(wq=wq, wk=wk, wv=wv, bq2=bq2, bk2=bk2, pb2=pb2, bvrow=bvrow,
                  iwl=iwl2, g1r=g1r, g1i=g1i, g2r=g2r, g2i=g2i, fbrow=fbrow,
                  idn=idn, idf=idf, eoh=eoh, onesrow=onesrow)
    in_maps = [dict(shared, x_own=np.ascontiguousarray(x[c * SH:(c + 1) * SH]))
               for c in range(NCORE)]
    res = run_bass_kernel_spmd(nc, in_maps, core_ids=list(range(NCORE)))
    y = np.concatenate([res.results[c]["y_own"] for c in range(NCORE)], axis=0)
    return np.ascontiguousarray(y.reshape(1, S, D))


# version 14
# speedup vs baseline: 1.0547x; 1.0547x over previous
import numpy as np
import ml_dtypes
from contextlib import ExitStack

import concourse.bass as bass
import concourse.mybir as mybir
from concourse.bass_utils import run_bass_kernel_spmd

F32 = mybir.dt.float32
I32 = mybir.dt.int32
BF16 = mybir.dt.bfloat16
ALU = mybir.AluOpType
AF = mybir.ActivationFunctionType
AX = mybir.AxisListType.X
BF = ml_dtypes.bfloat16

NCORE = 8
S = 16384
D = 1024
SH = S // NCORE
HALF = SH // 2
NSUP = 16
QT = 256
NQT = HALF // QT

INVPI = float(1.0 / np.pi)
PI_F32 = float(np.float32(np.pi))
HALFPI = float(np.float32(np.pi / 2))
P1C = 3.140625
P2C = float((np.float32(np.float64(np.pi) - P1C).view(np.uint32) & np.uint32(0xFFFFF000)).view(np.float32))
P3C = float(np.float32(np.float64(np.pi) - P1C - P2C))
H1C = float(np.float32(np.pi / 2))
H2C = float(np.float32(np.float64(np.pi / 2) - H1C))


def build_nc(dbg=frozenset()):
    nc = bass.Bass("TRN2", target_bir_lowering=False)

    x_h = nc.dram_tensor("x_own", [SH, D], F32, kind="ExternalInput").ap()
    wq_h = nc.dram_tensor("wq", [D, D], F32, kind="ExternalInput").ap()
    wk_h = nc.dram_tensor("wk", [D, D], F32, kind="ExternalInput").ap()
    wv_h = nc.dram_tensor("wv", [D, D], F32, kind="ExternalInput").ap()
    bq_h = nc.dram_tensor("bq2", [128, 8], F32, kind="ExternalInput").ap()
    bk_h = nc.dram_tensor("bk2", [128, 8], F32, kind="ExternalInput").ap()
    bv_h = nc.dram_tensor("bvrow", [1, D], F32, kind="ExternalInput").ap()
    pb_h = nc.dram_tensor("pb2", [128, 8], F32, kind="ExternalInput").ap()
    iwl_h = nc.dram_tensor("iwl", [128, 8], F32, kind="ExternalInput").ap()
    g1r_h = nc.dram_tensor("g1r", [4, 1024], F32, kind="ExternalInput").ap()
    g1i_h = nc.dram_tensor("g1i", [4, 1024], F32, kind="ExternalInput").ap()
    g2r_h = nc.dram_tensor("g2r", [4, 256], F32, kind="ExternalInput").ap()
    g2i_h = nc.dram_tensor("g2i", [4, 256], F32, kind="ExternalInput").ap()
    fb_h = nc.dram_tensor("fbrow", [1, D], F32, kind="ExternalInput").ap()
    idn_h = nc.dram_tensor("idn", [128, 128], BF16, kind="ExternalInput").ap()
    idf_h = nc.dram_tensor("idf", [8, 8], F32, kind="ExternalInput").ap()
    eoh_h = nc.dram_tensor("eoh", [128, 16], BF16, kind="ExternalInput").ap()
    ones_h = nc.dram_tensor("onesrow", [1, 128], BF16, kind="ExternalInput").ap()
    y_h = nc.dram_tensor("y_own", [SH, D], F32, kind="ExternalOutput").ap()

    wq_bf = nc.dram_tensor("wq_bf", [D, D], BF16)
    wk_bf = nc.dram_tensor("wk_bf", [D, D], BF16)
    wv_bf = nc.dram_tensor("wv_bf", [D, D], BF16)
    agin_ck = [nc.dram_tensor(f"agin_ck{i}", [D, SH // 2], BF16) for i in range(2)]
    agin_sk = [nc.dram_tensor(f"agin_sk{i}", [D, SH // 2], BF16) for i in range(2)]
    agin_v = [nc.dram_tensor(f"agin_v{i}", [SH // 2, D], BF16) for i in range(2)]
    ck_full = [nc.dram_tensor(f"ck_full{i}", [D * NCORE, SH // 2], BF16, addr_space="Shared") for i in range(2)]
    sk_full = [nc.dram_tensor(f"sk_full{i}", [D * NCORE, SH // 2], BF16, addr_space="Shared") for i in range(2)]
    v_full = [nc.dram_tensor(f"v_full{i}", [S // 2, D], BF16, addr_space="Shared") for i in range(2)]

    es = ExitStack()
    with es:
        sb = lambda n, sh, dt: es.enter_context(nc.sbuf_tensor(n, sh, dt))
        pp = lambda n, sh, dt: es.enter_context(nc.psum_tensor(n, sh, dt))

        wqT = sb("wqT", [128, 8 * 1024], BF16)
        wrT = sb("wrT", [128, 4 * 512], BF16)
        wiT = sb("wiT", [128, 4 * 512], BF16)
        nwiT = sb("nwiT", [128, 4 * 512], BF16)
        iwl = sb("iwl_t", [128, 8], F32)
        btq = sb("btq", [128, 8], F32)
        btk = sb("btk", [128, 8], F32)
        bq2 = sb("bq2_t", [128, 8], F32)
        bk2 = sb("bk2_t", [128, 8], F32)
        pb2 = sb("pb2_t", [128, 8], F32)
        bvr = sb("bvr", [1, D], BF16)
        fbr = sb("fbr", [1, D], BF16)
        idn = sb("idn_t", [128, 128], BF16)
        idf = sb("idf_t", [8, 8], F32)
        eoh = sb("eoh_t", [128, 16], BF16)
        onesr = sb("ones_t", [1, 128], BF16)

        xt = [sb(f"xt{i}", [128, D], F32) for i in range(2)]
        iwlp = sb("iwlp", [128, 8], F32)
        btqp = sb("btqp", [128, 8], F32)
        btkp = sb("btkp", [128, 8], F32)
        btqp2 = sb("btqp2", [128, 8], F32)
        btkp2 = sb("btkp2", [128, 8], F32)
        bnb8 = sb("bnb8", [128, 96], F32)
        bna8 = sb("bna8", [128, 16], F32)
        v8 = sb("v8", [128, 8], F32)
        isd8 = sb("isd8", [128, 8], F32)
        ST = {}

        def alloc_staging(stack, tag):
            sbx = lambda n, sh, dt: stack.enter_context(nc.sbuf_tensor(n + tag, sh, dt))
            ST["lnx"] = sbx("lnx", [128, 2 * D], BF16)
            ST["xg"] = sbx("xg", [128, 4 * 1024], F32)
            ST["lnxF"] = sbx("lnxF", [128, 8 * 512], BF16)
            ST["kA"] = sbx("kA", [128, 1024], I32)
            ST["kS"] = sbx("kS", [128, 2048], I32)
            ST["kC"] = sbx("kC", [128, 2048], I32)
            ST["rcT"] = sbx("rcT", [128, 1024], F32)
            ST["sgT"] = sbx("sgT", [128, 1024], BF16)
            ST["snT"] = sbx("snT", [128, 1024], BF16)
            ST["tht"] = sbx("tht", [128, 2048], F32)
            ST["cosb"] = sbx("cosb", [128, 1024], BF16)
            ST["sinb"] = sbx("sinb", [128, 1024], BF16)

        PS = {}

        def alloc_psum_build(stack, tag):
            ppx = lambda n, sh, dt: stack.enter_context(nc.psum_tensor(n + tag, sh, dt))
            PS["sc"] = [ppx(f"sc{i}", [128, 512], F32) for i in range(2)]
            PS["pT"] = [ppx(f"pT{i}", [128, 1024], BF16) for i in range(2)]

        def alloc_psum_attn(stack, tag):
            ppx = lambda n, sh, dt: stack.enter_context(nc.psum_tensor(n + tag, sh, dt))
            PS["pv"] = [ppx(f"pv{i}", [128, 512], F32) for i in range(4)]
            PS["sc"] = [ppx(f"sc{i}", [128, 512], F32) for i in range(3)]
            PS["sm"] = ppx("sums", [128, 512], F32)

        sem = {}
        for s_ in ("ld", "gd", "mm", "vv", "ac", "so", "cc", "xa", "xb", "tw", "sl", "s2", "s3", "s4"):
            sem[s_] = es.enter_context(nc.semaphore(s_))
        cnt = dict.fromkeys(sem, 0)

        block = es.enter_context(nc.Block())

        T = nc.tensor
        V = nc.vector
        A = nc.scalar
        SY = nc.sync
        G = nc.gpsimd

        def inc(instr, s_, amt=1):
            instr.then_inc(sem[s_], amt)
            cnt[s_] += amt
            return cnt[s_]

        def wait(eng, s_):
            eng.wait_ge(sem[s_], cnt[s_])

        def vop(instr):
            inc(instr, "vv")
            V.wait_ge(sem["vv"], cnt["vv"])

        def dma(eng, s_, out, in_, transpose=False):
            eng.wait_ge(sem[s_], cnt[s_])
            ins = (eng.dma_start_transpose(out=out, in_=in_) if transpose
                   else eng.dma_start(out=out, in_=in_))
            inc(ins, s_, 16)
            return cnt[s_]

        def emit_trig(th, cos_o, sin_o, pp_=0):
            kA = ST["kA"]; rcT = ST["rcT"]; sgT = ST["sgT"]; snT = ST["snT"]
            for mode in ("sin", "cos"):
                kin = (ST["kS"] if mode == "sin" else ST["kC"])[:, pp_ * 1024:(pp_ + 1) * 1024]
                vop(V.scalar_tensor_tensor(out=rcT[:], in0=kin[:], scalar=-PI_F32, in1=th, op0=ALU.mult, op1=ALU.add))
                hi = HALFPI if mode == "sin" else 0.0
                lo = -HALFPI if mode == "sin" else -PI_F32
                vop(V.scalar_tensor_tensor(out=kA[:], in0=rcT[:], scalar=hi, in1=kin[:], op0=ALU.is_gt, op1=ALU.add))
                vop(V.scalar_tensor_tensor(out=kA[:], in0=rcT[:], scalar=lo, in1=kA[:], op0=ALU.is_lt, op1=ALU.subtract))
                vop(V.scalar_tensor_tensor(out=rcT[:], in0=kA[:], scalar=P1C, in1=th, op0=ALU.mult, op1=ALU.add))
                vop(V.scalar_tensor_tensor(out=rcT[:], in0=kA[:], scalar=P2C, in1=rcT[:], op0=ALU.mult, op1=ALU.add))
                vop(V.scalar_tensor_tensor(out=rcT[:], in0=kA[:], scalar=P3C, in1=rcT[:], op0=ALU.mult, op1=ALU.add))
                if mode == "cos":
                    vop(V.tensor_scalar(out=rcT[:], in0=rcT[:], scalar1=H1C, scalar2=H2C, op0=ALU.add, op1=ALU.add))
                vop(V.tensor_scalar(out=kA[:], in0=kA[:], scalar1=1, scalar2=None, op0=ALU.bitwise_and))
                vop(V.tensor_scalar(out=sgT[:], in0=kA[:], scalar1=-2.0, scalar2=None, op0=ALU.mult))
                wait(A, "vv")
                acv = inc(A.activation(snT[:], rcT[:], AF.Sin), "ac")
                V.wait_ge(sem["ac"], acv)
                out = sin_o if mode == "sin" else cos_o
                vop(V.scalar_tensor_tensor(out=out, in0=sgT[:], scalar=1.0, in1=snT[:], op0=ALU.add, op1=ALU.mult))

        for dst, src in ((iwl, iwl_h), (bq2, bq_h), (bk2, bk_h), (pb2, pb_h),
                         (idn, idn_h), (idf, idf_h), (eoh, eoh_h), (onesr, ones_h)):
            dma(SY, "ld", dst[:], src[:, :])
        for wsrc, wdst in ((wq_h, wq_bf), (wk_h, wk_bf), (wv_h, wv_bf)):
            dma(G, "gd", wdst.ap()[:, :], wsrc[:, :])
        dma(G, "gd", bvr[:], bv_h[:, :])
        dma(G, "gd", fbr[:], fb_h[:, :])

        wait(V, "ld")
        vop(V.tensor_tensor(out=btq[:], in0=bq2[:], in1=iwl[:], op=ALU.mult))
        vop(V.tensor_tensor(out=btq[:], in0=btq[:], in1=pb2[:], op=ALU.add))
        vop(V.tensor_tensor(out=btk[:], in0=bk2[:], in1=iwl[:], op=ALU.mult))
        vop(V.tensor_tensor(out=btk[:], in0=btk[:], in1=pb2[:], op=ALU.add))
        vop(V.tensor_scalar(out=iwlp[:], in0=iwl[:], scalar1=INVPI, scalar2=None, op0=ALU.mult))
        vop(V.tensor_scalar(out=btqp[:], in0=btq[:], scalar1=INVPI, scalar2=None, op0=ALU.mult))
        vop(V.tensor_scalar(out=btkp[:], in0=btk[:], scalar1=INVPI, scalar2=None, op0=ALU.mult))
        vop(V.tensor_scalar(out=btqp2[:], in0=btqp[:], scalar1=1.0, scalar2=0.5, op0=ALU.mult, op1=ALU.add))
        vop(V.tensor_scalar(out=btkp2[:], in0=btkp[:], scalar1=1.0, scalar2=0.5, op0=ALU.mult, op1=ALU.add))
        bt_mark = cnt["vv"]

        cosq = sb("cosq", [128, 8 * 1024], BF16)
        sinq = sb("sinq", [128, 8 * 1024], BF16)
        es1 = ExitStack()
        sb1 = lambda n, sh, dt: es1.enter_context(nc.sbuf_tensor(n, sh, dt))
        alloc_staging(es1, "_p1")
        alloc_psum_build(es1, "_p1")
        pt_ac = [0, 0]
        xg_marks = [0, 0, 0, 0]
        xld_marks = [0, 0, 0, 0]
        trig_marks = [0, 0]
        g1r_t = sb1("g1r_t", [4, 1024], BF16)
        g1i_t = sb1("g1i_t", [4, 1024], BF16)
        g2r_t = sb1("g2r_t", [4, 256], BF16)
        g2i_t = sb1("g2i_t", [4, 256], BF16)
        outg = sb1("outg", [128, 1024], BF16)
        vevac = sb1("vevac", [128, D], BF16)
        wkT = sb1("wkT", [128, 8 * 1024], BF16)
        wvT = sb1("wvT", [128, 8 * 1024], BF16)
        for dst, src in ((g1r_t, g1r_h), (g1i_t, g1i_h), (g2r_t, g2r_h), (g2i_t, g2i_h)):
            dma(G, "gd", dst[:], src[:, :])
        wait(T, "gd")
        wait(T, "ld")

        for g1t, g2t, wdstT in (() if "p0f" in dbg else ((g1r_t, g2r_t, wrT), (g1i_t, g2i_t, wiT))):
            for C in range(4):
                for gl in range(4):
                    g = 4 * C + gl
                    stat = g2t[:, g::16]
                    sc = PS["sc"]
                    pT = PS["pT"][0]
                    T.matmul(sc[0][0:16, 0:512], stat, g1t[:, 0:512], start=True, stop=True)
                    mmv = inc(T.matmul(sc[1][0:16, 0:512], stat, g1t[:, 512:1024], start=True, stop=True), "mm")
                    V.wait_ge(sem["mm"], mmv)
                    vop(V.tensor_copy(outg[0:16, 0:512], sc[0][0:16, 0:512]))
                    vop(V.tensor_copy(outg[0:16, 512:1024], sc[1][0:16, 0:512]))
                    wait(T, "vv")
                    if "ffnw" not in dbg:
                        for b in range(32):
                            if gl < 3:
                                dst = pT[gl * 32:(gl + 1) * 32, b * 16:(b + 1) * 16]
                            else:
                                dst = pT[64:96, 512 + b * 16: 512 + (b + 1) * 16]
                            ins_ = T.matmul(dst, outg[0:16, b * 32:(b + 1) * 32],
                                            idn[0:16, 0:16],
                                            start=True, stop=True, is_transpose=True)
                        inc(ins_, "mm")
                    else:
                        inc(T.nop(), "mm")
                wait(V, "mm")
                vop(V.tensor_copy(wdstT[0:96, C * 512:(C + 1) * 512], PS["pT"][0][0:96, 0:512]))
                vop(V.tensor_copy(wdstT[96:128, C * 512:(C + 1) * 512], PS["pT"][0][64:96, 512:1024]))
                wait(T, "vv")
        vop(V.tensor_scalar(out=nwiT[:], in0=wiT[:], scalar1=-1.0, scalar2=None, op0=ALU.mult))


        def emit_xloads(tok0):
            xg = ST["xg"]
            for tt_ in range(4):
                psem = "xa" if tt_ % 2 == 0 else "xb"
                SY.wait_ge(sem["vv"], xg_marks[tt_])
                dma(SY, psem, xg[:, tt_ * 1024:(tt_ + 1) * 1024],
                    x_h[tok0 + tt_ * 128: tok0 + (tt_ + 1) * 128, :])
                xld_marks[tt_] = cnt[psem]

        def emit_group_build(tok0, proj_passes, write_v, next_tok0=None):
            lnx = ST["lnx"]; lnxF = ST["lnxF"]; tht = ST["tht"]; cosb = ST["cosb"]; sinb = ST["sinb"]
            xg = ST["xg"]
            for tt_ in range(4):
                psem = "xa" if tt_ % 2 == 0 else "xb"
                V.wait_ge(sem[psem], xld_marks[tt_])
                vop(V.bn_stats(bnb8[:, tt_ * 12: tt_ * 12 + 6], xg[:, tt_ * 1024: tt_ * 1024 + 512]))
                vop(V.bn_stats(bnb8[:, tt_ * 12 + 6: tt_ * 12 + 12], xg[:, tt_ * 1024 + 512: (tt_ + 1) * 1024]))
                vop(V.bn_aggr(bna8[:, tt_ * 2: tt_ * 2 + 2], bnb8[:, tt_ * 12: tt_ * 12 + 12]))
            vop(V.tensor_scalar(out=v8[:, 0:4], in0=bna8[:, 1:8:2], scalar1=1.0, scalar2=1e-5,
                                op0=ALU.mult, op1=ALU.add))
            vop(V.reciprocal(v8[:, 0:4], v8[:, 0:4]))
            wait(A, "vv")
            acv = inc(A.activation(isd8[:, 0:4], v8[:, 0:4], AF.Sqrt), "ac")
            V.wait_ge(sem["ac"], acv)
            lnx_mm = [0, 0]
            for tt_ in range(4):
                bk = tt_ % 2
                lx = lnx[:, bk * D:(bk + 1) * D]
                V.wait_ge(sem["mm"], lnx_mm[bk])
                vop(V.tensor_scalar(out=lx, in0=xg[:, tt_ * 1024:(tt_ + 1) * 1024],
                                    scalar1=bna8[:, tt_ * 2: tt_ * 2 + 1], scalar2=isd8[:, tt_: tt_ + 1],
                                    op0=ALU.subtract, op1=ALU.mult))
                xg_marks[tt_] = cnt["vv"]
                pTb = PS["pT"][bk]
                T.wait_ge(sem["ac"], pt_ac[bk])
                wait(T, "vv")
                for dchunk in range(8):
                    ins_ = T.matmul(pTb[:, dchunk * 128:(dchunk + 1) * 128],
                                    lx[:, dchunk * 128:(dchunk + 1) * 128], idn[:],
                                    start=True, stop=True, is_transpose=True)
                mmv = inc(ins_, "mm")
                lnx_mm[bk] = mmv
                A.wait_ge(sem["mm"], mmv)
                for dchunk in range(8):
                    acv = inc(A.activation(lnxF[:, dchunk * 512 + tt_ * 128: dchunk * 512 + (tt_ + 1) * 128],
                                           pTb[:, dchunk * 128:(dchunk + 1) * 128], AF.Identity), "ac")
                pt_ac[bk] = cnt["ac"]
            if next_tok0 is not None:
                emit_xloads(next_tok0)
            wait(T, "ac")

            wait(T, "ld")
            for (wT, bt, write_trig, cosq_dst, sinq_dst, qcol0) in proj_passes:
                sc = PS["sc"]
                btp, btp2 = (btqp, btqp2) if bt is btq else (btkp, btkp2)
                A.wait_ge(sem["vv"], bt_mark)
                for ec in range(8):
                    ps = sc[ec % 2]
                    for dc in range(8):
                        ins_ = T.matmul(ps[:, 0:512], wT[:, dc * 1024 + ec * 128: dc * 1024 + (ec + 1) * 128],
                                        lnxF[:, dc * 512:(dc + 1) * 512],
                                        start=(dc == 0), stop=(dc == 7))
                    mmv = inc(ins_, "mm")
                    A.wait_ge(sem["mm"], mmv)
                    pp_ = (ec // 2) % 2
                    if ec % 2 == 0:
                        A.wait_ge(sem["vv"], trig_marks[pp_])
                    half = pp_ * 1024 + (ec % 2) * 512
                    inc(A.activation(ST["kS"][:, half:half + 512], ps[:, 0:512], AF.Identity,
                                     bias=btp[:, ec:ec + 1], scale=iwlp[:, ec:ec + 1]), "ac")
                    inc(A.activation(ST["kC"][:, half:half + 512], ps[:, 0:512], AF.Identity,
                                     bias=btp2[:, ec:ec + 1], scale=iwlp[:, ec:ec + 1]), "ac")
                    acv = inc(A.activation(tht[:, half:half + 512], ps[:, 0:512],
                                           AF.Identity,
                                           bias=bt[:, ec:ec + 1], scale=iwl[:, ec:ec + 1]), "ac")
                    V.wait_ge(sem["ac"], acv)
                    T.wait_ge(sem["ac"], acv)
                    if ec % 2 == 0:
                        continue
                    tslice = tht[:, pp_ * 1024:(pp_ + 1) * 1024]
                    if cosq_dst is None:
                        wait(V, "tw")
                        emit_trig(tslice, cosb[:], sinb[:], pp_=pp_)
                        _, _, colbase = write_trig
                        gh_ = colbase // (SH // 2)
                        cb_ = colbase % (SH // 2)
                        wait(SY, "vv")
                        dma(SY, "tw", agin_ck[gh_].ap()[(ec - 1) * 128:(ec + 1) * 128, cb_:cb_ + 512]
                            .rearrange("(c p) q -> p c q", p=128),
                            cosb[:, :].rearrange("p (c q) -> p c q", c=2))
                        dma(SY, "tw", agin_sk[gh_].ap()[(ec - 1) * 128:(ec + 1) * 128, cb_:cb_ + 512]
                            .rearrange("(c p) q -> p c q", p=128),
                            sinb[:, :].rearrange("p (c q) -> p c q", c=2))
                    else:
                        cq3 = cosq_dst[:, :].rearrange("p (c q) -> p c q", c=8)
                        sq3 = sinq_dst[:, :].rearrange("p (c q) -> p c q", c=8)
                        emit_trig(tslice,
                                  cq3[:, ec - 1:ec + 1, qcol0:qcol0 + 512],
                                  sq3[:, ec - 1:ec + 1, qcol0:qcol0 + 512], pp_=pp_)
                    trig_marks[pp_] = cnt["vv"]

            if write_v is not None:
                sc = PS["sc"]
                for ts_ in range(4):
                    for eh in range(2):
                        ps = sc[eh]
                        for dc in range(8):
                            T.matmul(ps[:, 0:512],
                                     lnxF[:, dc * 512 + ts_ * 128: dc * 512 + (ts_ + 1) * 128],
                                     wvT[:, dc * 1024 + eh * 512: dc * 1024 + (eh + 1) * 512],
                                     start=(dc == 0), stop=False)
                        ins_ = T.matmul(ps[:, 0:512], onesr[0:1, 0:128],
                                        bvr[0:1, eh * 512:(eh + 1) * 512], start=False, stop=True)
                    mmv = inc(ins_, "mm")
                    A.wait_ge(sem["mm"], mmv)
                    wait(A, "tw")
                    inc(A.activation(vevac[:, 0:512], sc[0][:, 0:512], AF.Identity), "ac")
                    acv = inc(A.activation(vevac[:, 512:1024], sc[1][:, 0:512], AF.Identity), "ac")
                    T.wait_ge(sem["ac"], acv)
                    SY.wait_ge(sem["ac"], acv)
                    _vr = write_v + ts_ * 128
                    dma(SY, "tw", agin_v[_vr // (SH // 2)].ap()[_vr % (SH // 2): _vr % (SH // 2) + 128, :],
                        vevac[:])

        if "p1" not in dbg:
            emit_xloads(0)
        wait(SY, "gd")
        for wdst, wsrc in (() if "wtr" in dbg else ((wqT, wq_bf), (wkT, wk_bf), (wvT, wv_bf))):
            for dchunk in range(8):
                dma(SY, "ld", wdst[:, dchunk * 1024:(dchunk + 1) * 1024],
                    wsrc.ap()[:, dchunk * 128:(dchunk + 1) * 128], transpose=True)
        ag_marks = []
        if "p1" not in dbg:
            for grp in range(4):
                passes = [(wkT, btk, (agin_ck, agin_sk, grp * 512), None, None, 0)]
                emit_group_build(grp * 512, passes, write_v=grp * 512,
                                 next_tok0=(grp + 1) * 512 if grp < 3 else None)
                if grp % 2 == 1:
                    ag_marks.append(cnt["tw"])
        if "ag" not in dbg:
            for gh in range(2):
                G.wait_ge(sem["tw"], ag_marks[gh] if ag_marks else cnt["tw"])
                for ain, aout in ((agin_ck[gh], ck_full[gh]), (agin_sk[gh], sk_full[gh]),
                                  (agin_v[gh], v_full[gh])):
                    inc(G.collective_compute(
                        "AllGather", ALU.bypass, replica_groups=[list(range(NCORE))],
                        ins=[ain.ap().opt()], outs=[aout.ap().opt()]), "cc")
        es1.close()

        oacc = sb("oacc", [128, 8 * 1024], F32)
        pts = sb("pts", [128, 8 * 256], BF16)
        sums_sb = sb("sums_sb", [128, 256], F32)
        rinv = sb("rinv", [128, 8], F32)

        for h in range(2):
            hq0 = h * HALF
            wait(V, "mm")
            wait(V, "tw")
            es_a = ExitStack()
            alloc_staging(es_a, f"_h{h}")
            alloc_psum_build(es_a, f"_a{h}")
            pt_ac = [0, 0]
            xg_marks = [cnt["vv"]] * 4
            xld_marks = [0, 0, 0, 0]
            trig_marks = [cnt["vv"], cnt["vv"]]
            if "p2" not in dbg:
                wait(SY, "tw")
                wait(SY, "vv")
                wait(SY, "ac")
                wait(SY, "mm")
                emit_xloads(hq0)
                for grp in range(2):
                    emit_group_build(hq0 + grp * 512,
                                     [(wqT, btq, None, cosq, sinq, grp * 512)],
                                     write_v=None,
                                     next_tok0=hq0 + 512 if grp == 0 else None)
            es_a.close()

            wait(T, "vv")

            es_b = ExitStack()
            alloc_psum_attn(es_b, f"_b{h}")
            pv = PS["pv"]; sm = PS["sm"]
            sbb = lambda n, sh, dt: es_b.enter_context(nc.sbuf_tensor(n + f"_h{h}", sh, dt))
            ckb = sbb("ckb", [128, 2 * 8 * 1024], BF16)
            skb = sbb("skb", [128, 2 * 8 * 1024], BF16)
            vbb = sbb("vbb", [128, 2 * 8 * 1024], BF16)
            SLOT = 8 * 1024

            sup_marks = []
            cons_marks = []

            def load_super(s_g):
                if s_g >= 2:
                    SY.wait_ge(sem["mm"], cons_marks[s_g - 2])
                SY.wait_ge(sem["cc"], 3 if s_g % 2 == 0 else 6)
                r = s_g // 2
                so_ = (s_g % 2) * SLOT
                for hl in range(2):
                    lk = (s_g % 2) * 1024 + hl * 512
                    gh_ = lk // (SH // 2)
                    lo_ = lk % (SH // 2)
                    dma(SY, "sl",
                        ckb[:, so_:so_ + SLOT].rearrange("p (c k) -> p c k", c=8)[:, :, hl * 512:(hl + 1) * 512],
                        ck_full[gh_].ap()[r * 1024:(r + 1) * 1024, lo_: lo_ + 512]
                            .rearrange("(c p) k -> p c k", p=128))
                    dma(SY, "s2",
                        skb[:, so_:so_ + SLOT].rearrange("p (c k) -> p c k", c=8)[:, :, hl * 512:(hl + 1) * 512],
                        sk_full[gh_].ap()[r * 1024:(r + 1) * 1024, lo_: lo_ + 512]
                            .rearrange("(c p) k -> p c k", p=128))
                    dma(SY, "s3",
                        vbb[:, so_:so_ + SLOT].rearrange("p (c e) -> p c e", c=8)[:, hl * 4:(hl + 1) * 4, :],
                        v_full[gh_].ap()[r * (SH // 2) + lo_: r * (SH // 2) + lo_ + 512, :]
                            .rearrange("(c p) e -> p c e", p=128))
                sup_marks.append((cnt["sl"], cnt["s2"], cnt["s3"]))

            wait(SY, "mm")
            wait(SY, "vv")
            wait(SY, "ac")
            load_super(0)
            load_super(1)
            exp_acv = []
            gchunk = 0
            evac_vv = [0, 0, 0, 0]
            for sp in range(NSUP if "attn" not in dbg else 0):
                for qt in range(NQT):
                    for j in range(9):
                        if j < 8:
                            mk = sup_marks[sp]
                            T.wait_ge(sem["sl"], mk[0])
                            T.wait_ge(sem["s2"], mk[1])
                            T.wait_ge(sem["s3"], mk[2])
                            ps = PS["sc"][gchunk % 3]
                            if gchunk >= 3:
                                T.wait_ge(sem["ac"], exp_acv[gchunk - 3])
                            first = True
                            so_ = (sp % 2) * SLOT
                            for opa, opb in ((ckb, cosq), (skb, sinq)):
                                for dc in range(8):
                                    ins_ = T.matmul(ps[:, 0:256],
                                                    opa[:, so_ + dc * 1024 + j * 128: so_ + dc * 1024 + (j + 1) * 128],
                                                    opb[:, dc * 1024 + qt * 256: dc * 1024 + qt * 256 + 256],
                                                    start=first, stop=(opa is skb and dc == 7))
                                    first = False
                            mmv = inc(ins_, "mm")
                            A.wait_ge(sem["mm"], mmv)
                            acv = inc(A.activation(pts[:, j * 256:(j + 1) * 256], ps[:, 0:256],
                                                   AF.Exp, scale=1.0 / 32.0), "ac")
                            exp_acv.append(acv)
                            gchunk += 1
                        if j > 0:
                            jj = j - 1
                            T.wait_ge(sem["ac"], exp_acv[gchunk - (2 if j < 8 else 1)])
                            for qs in range(2):
                                for eh in range(2):
                                    if jj == 0:
                                        T.wait_ge(sem["vv"], evac_vv[qs * 2 + eh])
                                    T.matmul(pv[qs * 2 + eh][:, 0:512],
                                             pts[:, jj * 256 + qs * 128: jj * 256 + qs * 128 + 128],
                                             vbb[:, (sp % 2) * SLOT + jj * 1024 + eh * 512:
                                                 (sp % 2) * SLOT + jj * 1024 + (eh + 1) * 512],
                                             start=(jj == 0), stop=(jj == 7))
                            ins_ = T.matmul(sm[0:4, 0:256], eoh[:, qt * 4:(qt + 1) * 4],
                                            pts[:, jj * 256:(jj + 1) * 256],
                                            start=(sp == 0 and qt == 0 and jj == 0),
                                            stop=(sp == NSUP - 1 and qt == NQT - 1 and jj == 7))
                    mmv = inc(ins_, "mm")
                    if qt == NQT - 1:
                        cons_marks.append(mmv)
                        if sp + 2 < NSUP:
                            load_super(sp + 2)
                    V.wait_ge(sem["mm"], mmv)
                    for qs in range(2):
                        for eh in range(2):
                            dst = oacc[:, (qt * 2 + qs) * 1024 + eh * 512:
                                       (qt * 2 + qs) * 1024 + (eh + 1) * 512]
                            if sp == 0:
                                vop(V.tensor_copy(dst, pv[qs * 2 + eh][:, 0:512]))
                            else:
                                vop(V.tensor_tensor(out=dst, in0=dst, in1=pv[qs * 2 + eh][:, 0:512], op=ALU.add))
                            evac_vv[qs * 2 + eh] = cnt["vv"]

            wait(V, "mm")
            vop(V.tensor_copy(sums_sb[0:4, :], sm[0:4, 0:256]))
            wait(T, "vv")
            ins_ = T.nop()
            for qs_ in range(2 if "sumt" not in dbg else 0):
                ins_ = T.matmul(pv[0][0:128, qs_ * 4:(qs_ + 1) * 4],
                                sums_sb[0:4, qs_ * 128:(qs_ + 1) * 128],
                                idf[0:4, 0:4], start=(qs_ == 0), stop=(qs_ == 1), is_transpose=True)
            mmv = inc(ins_, "mm")
            V.wait_ge(sem["mm"], mmv)
            vop(V.reciprocal(rinv[:], pv[0][0:128, 0:8]))
            wait(T, "vv")
            es_b.close()

            es_c = ExitStack()
            alloc_psum_build(es_c, f"_c{h}")
            pt_ac = [0, 0]
            x1v = [cnt["vv"], cnt["vv"]]
            sbc = lambda n, sh, dt: es_c.enter_context(nc.sbuf_tensor(n + f"_h{h}", sh, dt))
            x1a = sbc("x1a", [128, 8 * 1024], F32)
            lnx2 = sbc("lnx2", [128, D], BF16)
            lnx2F = sbc("lnx2F", [128, 8 * 128], BF16)
            outt = [sbc("outt0", [128, D], F32), sbc("outt1", [128, D], F32)]

            for sub in range(8):
                buf = xt[sub % 2]
                psem = "xa" if sub % 2 == 0 else "xb"
                SY.wait_ge(sem["vv"], x1v[sub % 2])
                dma(SY, psem, buf[:], x_h[hq0 + sub * 128: hq0 + (sub + 1) * 128, :])
                wait(V, psem)
                rcol = (sub % 2) * 4 + (sub // 2)
                vop(V.scalar_tensor_tensor(out=x1a[:, sub * 1024:(sub + 1) * 1024],
                                           in0=oacc[:, sub * 1024:(sub + 1) * 1024],
                                           scalar=rinv[:, rcol:rcol + 1], in1=buf[:],
                                           op0=ALU.mult, op1=ALU.add))
                x1v[sub % 2] = cnt["vv"]
                vop(V.bn_stats(bnb8[:, sub * 12: sub * 12 + 6], x1a[:, sub * 1024: sub * 1024 + 512]))
                vop(V.bn_stats(bnb8[:, sub * 12 + 6: sub * 12 + 12], x1a[:, sub * 1024 + 512: (sub + 1) * 1024]))
                vop(V.bn_aggr(bna8[:, sub * 2: sub * 2 + 2], bnb8[:, sub * 12: sub * 12 + 12]))
            vop(V.tensor_scalar(out=v8[:], in0=bna8[:, 1::2], scalar1=1.0, scalar2=1e-5,
                                op0=ALU.mult, op1=ALU.add))
            vop(V.reciprocal(v8[:], v8[:]))
            wait(A, "vv")
            acv = inc(A.activation(isd8[:], v8[:], AF.Sqrt), "ac")
            V.wait_ge(sem["ac"], acv)

            for sub in range(8):
                rows = hq0 + sub * 128
                out_b = outt[sub % 2]
                osem = "so" if sub % 2 == 0 else "s4"
                x1s = x1a[:, sub * 1024:(sub + 1) * 1024]
                ln2ap = lnx2[:, :].rearrange("p (ih gh gl d) -> p ih gh gl d", ih=2, gh=4, gl=4, d=32)
                xin = x1s.rearrange("p (ih d gh gl) -> p ih gh gl d", ih=2, d=32, gh=4, gl=4)
                wait(V, "mm")
                vop(V.tensor_scalar(out=ln2ap, in0=xin, scalar1=bna8[:, sub * 2: sub * 2 + 1],
                                    scalar2=isd8[:, sub: sub + 1], op0=ALU.subtract, op1=ALU.mult))
                wait(T, "vv")
                T.wait_ge(sem["ac"], max(pt_ac))
                for ih in range(2):
                    pTb = PS["pT"][ih]
                    for C in range(4):
                        ins_ = T.matmul(pTb[:, C * 128:(C + 1) * 128],
                                        lnx2[:, (ih * 4 + C) * 128:(ih * 4 + C + 1) * 128],
                                        idn[:], start=True, stop=True, is_transpose=True)
                mmv = inc(ins_, "mm")
                A.wait_ge(sem["mm"], mmv)
                for cc_ in range(8):
                    acv = inc(A.activation(lnx2F[:, cc_ * 128:(cc_ + 1) * 128],
                                           PS["pT"][cc_ // 4][:, (cc_ % 4) * 128:(cc_ % 4 + 1) * 128],
                                           AF.Identity), "ac")
                pt_ac[0] = pt_ac[1] = cnt["ac"]
                wait(T, "ac")
                sc = PS["sc"]
                for C in range(4):
                    T.matmul(sc[0][:, 0:512], lnx2F[:, C * 128:(C + 1) * 128],
                             wrT[:, C * 512:(C + 1) * 512], start=(C == 0), stop=False)
                for C in range(4):
                    T.matmul(sc[0][:, 0:512], lnx2F[:, (4 + C) * 128:(4 + C + 1) * 128],
                             nwiT[:, C * 512:(C + 1) * 512], start=False, stop=False)
                T.matmul(sc[0][:, 0:512], onesr[0:1, 0:128], fbr[0:1, 0:512],
                         start=False, stop=True)
                for C in range(4):
                    T.matmul(sc[1][:, 0:512], lnx2F[:, C * 128:(C + 1) * 128],
                             wiT[:, C * 512:(C + 1) * 512], start=(C == 0), stop=False)
                for C in range(4):
                    T.matmul(sc[1][:, 0:512], lnx2F[:, (4 + C) * 128:(4 + C + 1) * 128],
                             wrT[:, C * 512:(C + 1) * 512], start=False, stop=False)
                mmv = inc(T.matmul(sc[1][:, 0:512], onesr[0:1, 0:128], fbr[0:1, 512:1024],
                                   start=False, stop=True), "mm")
                V.wait_ge(sem["mm"], mmv)
                wait(V, osem)
                vop(V.tensor_tensor(out=out_b[:, 0:512], in0=x1s[:, 0:512], in1=sc[0][:, 0:512], op=ALU.add))
                vop(V.tensor_tensor(out=out_b[:, 512:1024], in0=x1s[:, 512:1024], in1=sc[1][:, 0:512], op=ALU.add))
                wait(T, "vv")
                wait(SY, "vv")
                dma(SY, osem, y_h[rows:rows + 128, :], out_b[:])
            wait(V, "so")
            wait(V, "s4")
            wait(T, "so")
            wait(T, "s4")
            es_c.close()

        wait(SY, "so")
        wait(SY, "s4")
    return nc


_NC_CACHE = {}


def kernel(L0_attn_x, L0_attn_Wq, L0_attn_bq, L0_attn_Wk, L0_attn_bk, L0_attn_Wv,
           L0_attn_bv, L0_attn_phase_bias, L0_ffn_G1_R, L0_ffn_G2_R, L0_ffn_G1_I,
           L0_ffn_G2_I, L0_ffn_bias_real, L0_ffn_bias_imag):
    x = np.ascontiguousarray(np.asarray(L0_attn_x, dtype=np.float32).reshape(S, D))
    wq = np.ascontiguousarray(np.asarray(L0_attn_Wq, dtype=np.float32))
    wk = np.ascontiguousarray(np.asarray(L0_attn_Wk, dtype=np.float32))
    wv = np.ascontiguousarray(np.asarray(L0_attn_Wv, dtype=np.float32))
    bq2 = np.ascontiguousarray(np.asarray(L0_attn_bq, dtype=np.float32).reshape(8, 128).T)
    bk2 = np.ascontiguousarray(np.asarray(L0_attn_bk, dtype=np.float32).reshape(8, 128).T)
    pb2 = np.ascontiguousarray(np.asarray(L0_attn_phase_bias, dtype=np.float32).reshape(8, 128).T)
    bvrow = np.ascontiguousarray(np.asarray(L0_attn_bv, dtype=np.float32).reshape(1, D))
    wl = np.arange(1, D + 1, dtype=np.float64) * (2.0 * np.pi / D)
    iwl2 = np.ascontiguousarray((1.0 / (wl + 1e-8)).astype(np.float32).reshape(8, 128).T)
    g1r = np.ascontiguousarray(np.asarray(L0_ffn_G1_R, dtype=np.float32).reshape(32, 4, 32)
                               .transpose(1, 0, 2).reshape(4, 1024))
    g1i = np.ascontiguousarray(np.asarray(L0_ffn_G1_I, dtype=np.float32).reshape(32, 4, 32)
                               .transpose(1, 0, 2).reshape(4, 1024))
    g2r = np.ascontiguousarray(np.asarray(L0_ffn_G2_R, dtype=np.float32).reshape(4, 256))
    g2i = np.ascontiguousarray(np.asarray(L0_ffn_G2_I, dtype=np.float32).reshape(4, 256))
    fbrow = np.ascontiguousarray(
        np.concatenate([np.asarray(L0_ffn_bias_real, dtype=np.float32),
                        np.asarray(L0_ffn_bias_imag, dtype=np.float32)]).reshape(1, D))
    idn = np.eye(128, dtype=np.float32).astype(BF)
    idf = np.eye(8, dtype=np.float32)
    eoh = np.zeros((128, 16), dtype=np.float32)
    for t in range(4):
        eoh[:, 4 * t + t] = 1.0
    eoh = eoh.astype(BF)
    onesrow = np.ones((1, 128), dtype=np.float32).astype(BF)

    if "nc" not in _NC_CACHE:
        _NC_CACHE["nc"] = build_nc()
    nc = _NC_CACHE["nc"]

    shared = dict(wq=wq, wk=wk, wv=wv, bq2=bq2, bk2=bk2, pb2=pb2, bvrow=bvrow,
                  iwl=iwl2, g1r=g1r, g1i=g1i, g2r=g2r, g2i=g2i, fbrow=fbrow,
                  idn=idn, idf=idf, eoh=eoh, onesrow=onesrow)
    in_maps = [dict(shared, x_own=np.ascontiguousarray(x[c * SH:(c + 1) * SH]))
               for c in range(NCORE)]
    res = run_bass_kernel_spmd(nc, in_maps, core_ids=list(range(NCORE)))
    y = np.concatenate([res.results[c]["y_own"] for c in range(NCORE)], axis=0)
    return np.ascontiguousarray(y.reshape(1, S, D))


# version 16 (final)
# speedup vs baseline: 1.0751x; 1.0193x over previous
import numpy as np
import ml_dtypes
from contextlib import ExitStack

import concourse.bass as bass
import concourse.mybir as mybir
from concourse.bass_utils import run_bass_kernel_spmd

F32 = mybir.dt.float32
I32 = mybir.dt.int32
BF16 = mybir.dt.bfloat16
ALU = mybir.AluOpType
AF = mybir.ActivationFunctionType
AX = mybir.AxisListType.X
BF = ml_dtypes.bfloat16

NCORE = 8
S = 16384
D = 1024
SH = S // NCORE
HALF = SH // 2
NSUP = 16
QT = 256
NQT = HALF // QT

INVPI = float(1.0 / np.pi)
PI_F32 = float(np.float32(np.pi))
HALFPI = float(np.float32(np.pi / 2))
P1C = 3.140625
P2C = float(np.float32(np.float64(np.pi) - P1C))

H1C = float(np.float32(np.pi / 2))
H2C = float(np.float32(np.float64(np.pi / 2) - H1C))


def build_nc(dbg=frozenset()):
    nc = bass.Bass("TRN2", target_bir_lowering=False)

    x_h = nc.dram_tensor("x_own", [SH, D], F32, kind="ExternalInput").ap()
    wq_h = nc.dram_tensor("wq", [D, D], F32, kind="ExternalInput").ap()
    wk_h = nc.dram_tensor("wk", [D, D], F32, kind="ExternalInput").ap()
    wv_h = nc.dram_tensor("wv", [D, D], F32, kind="ExternalInput").ap()
    bq_h = nc.dram_tensor("bq2", [128, 8], F32, kind="ExternalInput").ap()
    bk_h = nc.dram_tensor("bk2", [128, 8], F32, kind="ExternalInput").ap()
    bv_h = nc.dram_tensor("bvrow", [1, D], F32, kind="ExternalInput").ap()
    pb_h = nc.dram_tensor("pb2", [128, 8], F32, kind="ExternalInput").ap()
    iwl_h = nc.dram_tensor("iwl", [128, 8], F32, kind="ExternalInput").ap()
    g1r_h = nc.dram_tensor("g1r", [4, 1024], F32, kind="ExternalInput").ap()
    g1i_h = nc.dram_tensor("g1i", [4, 1024], F32, kind="ExternalInput").ap()
    g2r_h = nc.dram_tensor("g2r", [4, 256], F32, kind="ExternalInput").ap()
    g2i_h = nc.dram_tensor("g2i", [4, 256], F32, kind="ExternalInput").ap()
    fb_h = nc.dram_tensor("fbrow", [1, D], F32, kind="ExternalInput").ap()
    idn_h = nc.dram_tensor("idn", [128, 128], BF16, kind="ExternalInput").ap()
    idf_h = nc.dram_tensor("idf", [8, 8], F32, kind="ExternalInput").ap()
    eoh_h = nc.dram_tensor("eoh", [128, 16], BF16, kind="ExternalInput").ap()
    ones_h = nc.dram_tensor("onesrow", [1, 128], BF16, kind="ExternalInput").ap()
    y_h = nc.dram_tensor("y_own", [SH, D], F32, kind="ExternalOutput").ap()

    wq_bf = nc.dram_tensor("wq_bf", [D, D], BF16)
    wk_bf = nc.dram_tensor("wk_bf", [D, D], BF16)
    wv_bf = nc.dram_tensor("wv_bf", [D, D], BF16)
    agin_ck = [nc.dram_tensor(f"agin_ck{i}", [D, SH // 2], BF16) for i in range(2)]
    agin_sk = [nc.dram_tensor(f"agin_sk{i}", [D, SH // 2], BF16) for i in range(2)]
    agin_v = [nc.dram_tensor(f"agin_v{i}", [SH // 2, D], BF16) for i in range(2)]
    ck_full = [nc.dram_tensor(f"ck_full{i}", [D * NCORE, SH // 2], BF16, addr_space="Shared") for i in range(2)]
    sk_full = [nc.dram_tensor(f"sk_full{i}", [D * NCORE, SH // 2], BF16, addr_space="Shared") for i in range(2)]
    v_full = [nc.dram_tensor(f"v_full{i}", [S // 2, D], BF16, addr_space="Shared") for i in range(2)]

    es = ExitStack()
    with es:
        sb = lambda n, sh, dt: es.enter_context(nc.sbuf_tensor(n, sh, dt))
        pp = lambda n, sh, dt: es.enter_context(nc.psum_tensor(n, sh, dt))

        wqT = sb("wqT", [128, 8 * 1024], BF16)
        wrT = sb("wrT", [128, 4 * 512], BF16)
        wiT = sb("wiT", [128, 4 * 512], BF16)
        nwiT = sb("nwiT", [128, 4 * 512], BF16)
        iwl = sb("iwl_t", [128, 8], F32)
        btq = sb("btq", [128, 8], F32)
        btk = sb("btk", [128, 8], F32)
        bq2 = sb("bq2_t", [128, 8], F32)
        bk2 = sb("bk2_t", [128, 8], F32)
        pb2 = sb("pb2_t", [128, 8], F32)
        bvr = sb("bvr", [1, D], BF16)
        fbr = sb("fbr", [1, D], BF16)
        idn = sb("idn_t", [128, 128], BF16)
        idf = sb("idf_t", [8, 8], F32)
        eoh = sb("eoh_t", [128, 16], BF16)
        onesr = sb("ones_t", [1, 128], BF16)

        xt = [sb(f"xt{i}", [128, D], F32) for i in range(2)]
        iwlp = sb("iwlp", [128, 8], F32)
        btqp = sb("btqp", [128, 8], F32)
        btkp = sb("btkp", [128, 8], F32)
        btqp2 = sb("btqp2", [128, 8], F32)
        btkp2 = sb("btkp2", [128, 8], F32)
        bnb8 = sb("bnb8", [128, 96], F32)
        bna8 = sb("bna8", [128, 16], F32)
        v8 = sb("v8", [128, 8], F32)
        isd8 = sb("isd8", [128, 8], F32)
        ST = {}

        def alloc_staging(stack, tag):
            sbx = lambda n, sh, dt: stack.enter_context(nc.sbuf_tensor(n + tag, sh, dt))
            ST["lnx"] = sbx("lnx", [128, 2 * D], BF16)
            ST["xg"] = sbx("xg", [128, 4 * 1024], F32)
            ST["lnxF"] = sbx("lnxF", [128, 8 * 512], BF16)
            ST["kA"] = sbx("kA", [128, 1024], I32)
            ST["kS"] = sbx("kS", [128, 2048], I32)
            ST["kC"] = sbx("kC", [128, 2048], I32)
            ST["rcT"] = sbx("rcT", [128, 1024], F32)
            ST["sgT"] = sbx("sgT", [128, 1024], BF16)
            ST["snT"] = sbx("snT", [128, 1024], BF16)
            ST["tht"] = sbx("tht", [128, 2048], F32)
            ST["cosb"] = sbx("cosb", [128, 2048], BF16)
            ST["sinb"] = sbx("sinb", [128, 2048], BF16)

        PS = {}

        def alloc_psum_build(stack, tag):
            ppx = lambda n, sh, dt: stack.enter_context(nc.psum_tensor(n + tag, sh, dt))
            PS["sc"] = [ppx(f"sc{i}", [128, 512], F32) for i in range(2)]
            PS["pT"] = [ppx(f"pT{i}", [128, 1024], BF16) for i in range(2)]

        def alloc_psum_attn(stack, tag):
            ppx = lambda n, sh, dt: stack.enter_context(nc.psum_tensor(n + tag, sh, dt))
            PS["pv"] = [ppx(f"pv{i}", [128, 512], F32) for i in range(4)]
            PS["sc"] = [ppx(f"sc{i}", [128, 512], F32) for i in range(3)]
            PS["sm"] = ppx("sums", [128, 512], F32)

        sem = {}
        for s_ in ("ld", "gd", "mm", "vv", "ac", "so", "cc", "xa", "xb", "tw", "vw", "sl", "s2", "s3", "s4"):
            sem[s_] = es.enter_context(nc.semaphore(s_))
        cnt = dict.fromkeys(sem, 0)

        block = es.enter_context(nc.Block())

        T = nc.tensor
        V = nc.vector
        A = nc.scalar
        SY = nc.sync
        G = nc.gpsimd

        def inc(instr, s_, amt=1):
            instr.then_inc(sem[s_], amt)
            cnt[s_] += amt
            return cnt[s_]

        def wait(eng, s_):
            eng.wait_ge(sem[s_], cnt[s_])

        def vop(instr):
            inc(instr, "vv")
            V.wait_ge(sem["vv"], cnt["vv"])

        def dma(eng, s_, out, in_, transpose=False):
            eng.wait_ge(sem[s_], cnt[s_])
            ins = (eng.dma_start_transpose(out=out, in_=in_) if transpose
                   else eng.dma_start(out=out, in_=in_))
            inc(ins, s_, 16)
            return cnt[s_]

        def emit_trig(th, cos_o, sin_o, pp_=0):
            kA = ST["kA"]; rcT = ST["rcT"]; sgT = ST["sgT"]; snT = ST["snT"]
            for mode in ("sin", "cos"):
                kin = (ST["kS"] if mode == "sin" else ST["kC"])[:, pp_ * 1024:(pp_ + 1) * 1024]
                vop(V.scalar_tensor_tensor(out=rcT[:], in0=kin[:], scalar=-PI_F32, in1=th, op0=ALU.mult, op1=ALU.add))
                hi = HALFPI if mode == "sin" else 0.0
                lo = -HALFPI if mode == "sin" else -PI_F32
                vop(V.scalar_tensor_tensor(out=kA[:], in0=rcT[:], scalar=hi, in1=kin[:], op0=ALU.is_gt, op1=ALU.add))
                vop(V.scalar_tensor_tensor(out=kA[:], in0=rcT[:], scalar=lo, in1=kA[:], op0=ALU.is_lt, op1=ALU.subtract))
                vop(V.scalar_tensor_tensor(out=rcT[:], in0=kA[:], scalar=P1C, in1=th, op0=ALU.mult, op1=ALU.add))
                vop(V.scalar_tensor_tensor(out=rcT[:], in0=kA[:], scalar=P2C, in1=rcT[:], op0=ALU.mult, op1=ALU.add))
                if mode == "cos":
                    vop(V.tensor_scalar(out=rcT[:], in0=rcT[:], scalar1=H1C, scalar2=H2C, op0=ALU.add, op1=ALU.add))
                vop(V.tensor_scalar(out=kA[:], in0=kA[:], scalar1=1, scalar2=None, op0=ALU.bitwise_and))
                vop(V.tensor_scalar(out=sgT[:], in0=kA[:], scalar1=-2.0, scalar2=None, op0=ALU.mult))
                wait(A, "vv")
                acv = inc(A.activation(snT[:], rcT[:], AF.Sin), "ac")
                V.wait_ge(sem["ac"], acv)
                out = sin_o if mode == "sin" else cos_o
                vop(V.scalar_tensor_tensor(out=out, in0=sgT[:], scalar=1.0, in1=snT[:], op0=ALU.add, op1=ALU.mult))

        for dst, src in ((iwl, iwl_h), (bq2, bq_h), (bk2, bk_h), (pb2, pb_h),
                         (idn, idn_h), (idf, idf_h), (eoh, eoh_h), (onesr, ones_h)):
            dma(SY, "ld", dst[:], src[:, :])
        for wsrc, wdst in ((wq_h, wq_bf), (wk_h, wk_bf), (wv_h, wv_bf)):
            dma(G, "gd", wdst.ap()[:, :], wsrc[:, :])
        dma(G, "gd", bvr[:], bv_h[:, :])
        dma(G, "gd", fbr[:], fb_h[:, :])

        wait(V, "ld")
        vop(V.tensor_tensor(out=btq[:], in0=bq2[:], in1=iwl[:], op=ALU.mult))
        vop(V.tensor_tensor(out=btq[:], in0=btq[:], in1=pb2[:], op=ALU.add))
        vop(V.tensor_tensor(out=btk[:], in0=bk2[:], in1=iwl[:], op=ALU.mult))
        vop(V.tensor_tensor(out=btk[:], in0=btk[:], in1=pb2[:], op=ALU.add))
        vop(V.tensor_scalar(out=iwlp[:], in0=iwl[:], scalar1=INVPI, scalar2=None, op0=ALU.mult))
        vop(V.tensor_scalar(out=btqp[:], in0=btq[:], scalar1=INVPI, scalar2=None, op0=ALU.mult))
        vop(V.tensor_scalar(out=btkp[:], in0=btk[:], scalar1=INVPI, scalar2=None, op0=ALU.mult))
        vop(V.tensor_scalar(out=btqp2[:], in0=btqp[:], scalar1=1.0, scalar2=0.5, op0=ALU.mult, op1=ALU.add))
        vop(V.tensor_scalar(out=btkp2[:], in0=btkp[:], scalar1=1.0, scalar2=0.5, op0=ALU.mult, op1=ALU.add))
        bt_mark = cnt["vv"]

        cosq = sb("cosq", [128, 8 * 1024], BF16)
        sinq = sb("sinq", [128, 8 * 1024], BF16)
        es1 = ExitStack()
        sb1 = lambda n, sh, dt: es1.enter_context(nc.sbuf_tensor(n, sh, dt))
        alloc_staging(es1, "_p1")
        alloc_psum_build(es1, "_p1")
        pt_ac = [0, 0]
        xg_marks = [0, 0, 0, 0]
        xld_marks = [0, 0, 0, 0]
        trig_marks = [0, 0]
        tw_marks = [0, 0]
        g1r_t = sb1("g1r_t", [4, 1024], BF16)
        g1i_t = sb1("g1i_t", [4, 1024], BF16)
        g2r_t = sb1("g2r_t", [4, 256], BF16)
        g2i_t = sb1("g2i_t", [4, 256], BF16)
        outg = sb1("outg", [128, 1024], BF16)
        vevac = sb1("vevac", [128, D], BF16)
        wkT = sb1("wkT", [128, 8 * 1024], BF16)
        wvT = sb1("wvT", [128, 8 * 1024], BF16)
        for dst, src in ((g1r_t, g1r_h), (g1i_t, g1i_h), (g2r_t, g2r_h), (g2i_t, g2i_h)):
            dma(G, "gd", dst[:], src[:, :])
        wait(T, "gd")
        wait(T, "ld")

        for g1t, g2t, wdstT in (() if "p0f" in dbg else ((g1r_t, g2r_t, wrT), (g1i_t, g2i_t, wiT))):
            for C in range(4):
                for gl in range(4):
                    g = 4 * C + gl
                    stat = g2t[:, g::16]
                    sc = PS["sc"]
                    pT = PS["pT"][0]
                    T.matmul(sc[0][0:16, 0:512], stat, g1t[:, 0:512], start=True, stop=True)
                    mmv = inc(T.matmul(sc[1][0:16, 0:512], stat, g1t[:, 512:1024], start=True, stop=True), "mm")
                    V.wait_ge(sem["mm"], mmv)
                    vop(V.tensor_copy(outg[0:16, 0:512], sc[0][0:16, 0:512]))
                    vop(V.tensor_copy(outg[0:16, 512:1024], sc[1][0:16, 0:512]))
                    wait(T, "vv")
                    if "ffnw" not in dbg:
                        for b in range(32):
                            if gl < 3:
                                dst = pT[gl * 32:(gl + 1) * 32, b * 16:(b + 1) * 16]
                            else:
                                dst = pT[64:96, 512 + b * 16: 512 + (b + 1) * 16]
                            ins_ = T.matmul(dst, outg[0:16, b * 32:(b + 1) * 32],
                                            idn[0:16, 0:16],
                                            start=True, stop=True, is_transpose=True)
                        inc(ins_, "mm")
                    else:
                        inc(T.nop(), "mm")
                wait(V, "mm")
                vop(V.tensor_copy(wdstT[0:96, C * 512:(C + 1) * 512], PS["pT"][0][0:96, 0:512]))
                vop(V.tensor_copy(wdstT[96:128, C * 512:(C + 1) * 512], PS["pT"][0][64:96, 512:1024]))
                wait(T, "vv")
        vop(V.tensor_scalar(out=nwiT[:], in0=wiT[:], scalar1=-1.0, scalar2=None, op0=ALU.mult))


        def emit_xloads(tok0):
            xg = ST["xg"]
            for tt_ in range(4):
                psem = "xa" if tt_ % 2 == 0 else "xb"
                SY.wait_ge(sem["vv"], xg_marks[tt_])
                dma(SY, psem, xg[:, tt_ * 1024:(tt_ + 1) * 1024],
                    x_h[tok0 + tt_ * 128: tok0 + (tt_ + 1) * 128, :])
                xld_marks[tt_] = cnt[psem]

        def emit_group_build(tok0, proj_passes, write_v, next_tok0=None):
            lnx = ST["lnx"]; lnxF = ST["lnxF"]; tht = ST["tht"]; cosb = ST["cosb"]; sinb = ST["sinb"]
            xg = ST["xg"]
            for tt_ in range(4):
                psem = "xa" if tt_ % 2 == 0 else "xb"
                V.wait_ge(sem[psem], xld_marks[tt_])
                vop(V.bn_stats(bnb8[:, tt_ * 12: tt_ * 12 + 6], xg[:, tt_ * 1024: tt_ * 1024 + 512]))
                vop(V.bn_stats(bnb8[:, tt_ * 12 + 6: tt_ * 12 + 12], xg[:, tt_ * 1024 + 512: (tt_ + 1) * 1024]))
                vop(V.bn_aggr(bna8[:, tt_ * 2: tt_ * 2 + 2], bnb8[:, tt_ * 12: tt_ * 12 + 12]))
            vop(V.tensor_scalar(out=v8[:, 0:4], in0=bna8[:, 1:8:2], scalar1=1.0, scalar2=1e-5,
                                op0=ALU.mult, op1=ALU.add))
            vop(V.reciprocal(v8[:, 0:4], v8[:, 0:4]))
            wait(A, "vv")
            acv = inc(A.activation(isd8[:, 0:4], v8[:, 0:4], AF.Sqrt), "ac")
            V.wait_ge(sem["ac"], acv)
            lnx_mm = [0, 0]
            for tt_ in range(4):
                bk = tt_ % 2
                lx = lnx[:, bk * D:(bk + 1) * D]
                V.wait_ge(sem["mm"], lnx_mm[bk])
                vop(V.tensor_scalar(out=lx, in0=xg[:, tt_ * 1024:(tt_ + 1) * 1024],
                                    scalar1=bna8[:, tt_ * 2: tt_ * 2 + 1], scalar2=isd8[:, tt_: tt_ + 1],
                                    op0=ALU.subtract, op1=ALU.mult))
                xg_marks[tt_] = cnt["vv"]
                pTb = PS["pT"][bk]
                T.wait_ge(sem["ac"], pt_ac[bk])
                wait(T, "vv")
                for dchunk in range(8):
                    ins_ = T.matmul(pTb[:, dchunk * 128:(dchunk + 1) * 128],
                                    lx[:, dchunk * 128:(dchunk + 1) * 128], idn[:],
                                    start=True, stop=True, is_transpose=True)
                mmv = inc(ins_, "mm")
                lnx_mm[bk] = mmv
                A.wait_ge(sem["mm"], mmv)
                for dchunk in range(8):
                    acv = inc(A.activation(lnxF[:, dchunk * 512 + tt_ * 128: dchunk * 512 + (tt_ + 1) * 128],
                                           pTb[:, dchunk * 128:(dchunk + 1) * 128], AF.Identity), "ac")
                pt_ac[bk] = cnt["ac"]
            if next_tok0 is not None:
                emit_xloads(next_tok0)
            wait(T, "ac")

            wait(T, "ld")
            for (wT, bt, write_trig, cosq_dst, sinq_dst, qcol0) in proj_passes:
                sc = PS["sc"]
                btp, btp2 = (btqp, btqp2) if bt is btq else (btkp, btkp2)
                A.wait_ge(sem["vv"], bt_mark)
                for ec in range(8):
                    ps = sc[ec % 2]
                    for dc in range(8):
                        ins_ = T.matmul(ps[:, 0:512], wT[:, dc * 1024 + ec * 128: dc * 1024 + (ec + 1) * 128],
                                        lnxF[:, dc * 512:(dc + 1) * 512],
                                        start=(dc == 0), stop=(dc == 7))
                    mmv = inc(ins_, "mm")
                    A.wait_ge(sem["mm"], mmv)
                    pp_ = (ec // 2) % 2
                    if ec % 2 == 0:
                        A.wait_ge(sem["vv"], trig_marks[pp_])
                    half = pp_ * 1024 + (ec % 2) * 512
                    inc(A.activation(ST["kS"][:, half:half + 512], ps[:, 0:512], AF.Identity,
                                     bias=btp[:, ec:ec + 1], scale=iwlp[:, ec:ec + 1]), "ac")
                    inc(A.activation(ST["kC"][:, half:half + 512], ps[:, 0:512], AF.Identity,
                                     bias=btp2[:, ec:ec + 1], scale=iwlp[:, ec:ec + 1]), "ac")
                    acv = inc(A.activation(tht[:, half:half + 512], ps[:, 0:512],
                                           AF.Identity,
                                           bias=bt[:, ec:ec + 1], scale=iwl[:, ec:ec + 1]), "ac")
                    V.wait_ge(sem["ac"], acv)
                    T.wait_ge(sem["ac"], acv)
                    if ec % 2 == 0:
                        continue
                    tslice = tht[:, pp_ * 1024:(pp_ + 1) * 1024]
                    if cosq_dst is None:
                        V.wait_ge(sem["tw"], tw_marks[pp_])
                        cbs = cosb[:, pp_ * 1024:(pp_ + 1) * 1024]
                        sbs = sinb[:, pp_ * 1024:(pp_ + 1) * 1024]
                        emit_trig(tslice, cbs, sbs, pp_=pp_)
                        _, _, colbase = write_trig
                        gh_ = colbase // (SH // 2)
                        cb_ = colbase % (SH // 2)
                        wait(SY, "vv")
                        dma(SY, "tw", agin_ck[gh_].ap()[(ec - 1) * 128:(ec + 1) * 128, cb_:cb_ + 512]
                            .rearrange("(c p) q -> p c q", p=128),
                            cbs.rearrange("p (c q) -> p c q", c=2))
                        dma(SY, "tw", agin_sk[gh_].ap()[(ec - 1) * 128:(ec + 1) * 128, cb_:cb_ + 512]
                            .rearrange("(c p) q -> p c q", p=128),
                            sbs.rearrange("p (c q) -> p c q", c=2))
                        tw_marks[pp_] = cnt["tw"]
                    else:
                        cq3 = cosq_dst[:, :].rearrange("p (c q) -> p c q", c=8)
                        sq3 = sinq_dst[:, :].rearrange("p (c q) -> p c q", c=8)
                        emit_trig(tslice,
                                  cq3[:, ec - 1:ec + 1, qcol0:qcol0 + 512],
                                  sq3[:, ec - 1:ec + 1, qcol0:qcol0 + 512], pp_=pp_)
                    trig_marks[pp_] = cnt["vv"]

            if write_v is not None:
                sc = PS["sc"]
                for ts_ in range(4):
                    for eh in range(2):
                        ps = sc[eh]
                        for dc in range(8):
                            T.matmul(ps[:, 0:512],
                                     lnxF[:, dc * 512 + ts_ * 128: dc * 512 + (ts_ + 1) * 128],
                                     wvT[:, dc * 1024 + eh * 512: dc * 1024 + (eh + 1) * 512],
                                     start=(dc == 0), stop=False)
                        ins_ = T.matmul(ps[:, 0:512], onesr[0:1, 0:128],
                                        bvr[0:1, eh * 512:(eh + 1) * 512], start=False, stop=True)
                    mmv = inc(ins_, "mm")
                    A.wait_ge(sem["mm"], mmv)
                    wait(A, "vw")
                    inc(A.activation(vevac[:, 0:512], sc[0][:, 0:512], AF.Identity), "ac")
                    acv = inc(A.activation(vevac[:, 512:1024], sc[1][:, 0:512], AF.Identity), "ac")
                    T.wait_ge(sem["ac"], acv)
                    SY.wait_ge(sem["ac"], acv)
                    _vr = write_v + ts_ * 128
                    dma(SY, "vw", agin_v[_vr // (SH // 2)].ap()[_vr % (SH // 2): _vr % (SH // 2) + 128, :],
                        vevac[:])

        if "p1" not in dbg:
            emit_xloads(0)
        wait(SY, "gd")
        for wdst, wsrc in (() if "wtr" in dbg else ((wqT, wq_bf), (wkT, wk_bf), (wvT, wv_bf))):
            for dchunk in range(8):
                dma(SY, "ld", wdst[:, dchunk * 1024:(dchunk + 1) * 1024],
                    wsrc.ap()[:, dchunk * 128:(dchunk + 1) * 128], transpose=True)
        ag_marks = []
        if "p1" not in dbg:
            for grp in range(4):
                passes = [(wkT, btk, (agin_ck, agin_sk, grp * 512), None, None, 0)]
                emit_group_build(grp * 512, passes, write_v=grp * 512,
                                 next_tok0=(grp + 1) * 512 if grp < 3 else None)
                if grp % 2 == 1:
                    ag_marks.append((cnt["tw"], cnt["vw"]))
        if "ag" not in dbg:
            for gh in range(2):
                G.wait_ge(sem["tw"], ag_marks[gh][0] if ag_marks else cnt["tw"])
                G.wait_ge(sem["vw"], ag_marks[gh][1] if ag_marks else cnt["vw"])
                for ain, aout in ((agin_ck[gh], ck_full[gh]), (agin_sk[gh], sk_full[gh]),
                                  (agin_v[gh], v_full[gh])):
                    inc(G.collective_compute(
                        "AllGather", ALU.bypass, replica_groups=[list(range(NCORE))],
                        ins=[ain.ap().opt()], outs=[aout.ap().opt()]), "cc")
        es1.close()

        oacc = sb("oacc", [128, 8 * 1024], F32)
        pts = sb("pts", [128, 8 * 256], BF16)
        sums_sb = sb("sums_sb", [128, 256], F32)
        rinv = sb("rinv", [128, 8], F32)

        for h in range(2):
            hq0 = h * HALF
            wait(V, "mm")
            wait(V, "tw")
            es_a = ExitStack()
            alloc_staging(es_a, f"_h{h}")
            alloc_psum_build(es_a, f"_a{h}")
            pt_ac = [0, 0]
            xg_marks = [cnt["vv"]] * 4
            xld_marks = [0, 0, 0, 0]
            trig_marks = [cnt["vv"], cnt["vv"]]
            tw_marks = [cnt["tw"], cnt["tw"]]
            if "p2" not in dbg:
                wait(SY, "tw")
                wait(SY, "vw")
                wait(SY, "vv")
                wait(SY, "ac")
                wait(SY, "mm")
                emit_xloads(hq0)
                for grp in range(2):
                    emit_group_build(hq0 + grp * 512,
                                     [(wqT, btq, None, cosq, sinq, grp * 512)],
                                     write_v=None,
                                     next_tok0=hq0 + 512 if grp == 0 else None)
            es_a.close()

            wait(T, "vv")

            es_b = ExitStack()
            alloc_psum_attn(es_b, f"_b{h}")
            pv = PS["pv"]; sm = PS["sm"]
            sbb = lambda n, sh, dt: es_b.enter_context(nc.sbuf_tensor(n + f"_h{h}", sh, dt))
            ckb = sbb("ckb", [128, 2 * 8 * 1024], BF16)
            skb = sbb("skb", [128, 2 * 8 * 1024], BF16)
            vbb = sbb("vbb", [128, 2 * 8 * 1024], BF16)
            SLOT = 8 * 1024

            sup_marks = []
            cons_marks = []

            def load_super(s_g):
                if s_g >= 2:
                    SY.wait_ge(sem["mm"], cons_marks[s_g - 2])
                SY.wait_ge(sem["cc"], 3 if s_g % 2 == 0 else 6)
                r = s_g // 2
                so_ = (s_g % 2) * SLOT
                for hl in range(2):
                    lk = (s_g % 2) * 1024 + hl * 512
                    gh_ = lk // (SH // 2)
                    lo_ = lk % (SH // 2)
                    dma(SY, "sl",
                        ckb[:, so_:so_ + SLOT].rearrange("p (c k) -> p c k", c=8)[:, :, hl * 512:(hl + 1) * 512],
                        ck_full[gh_].ap()[r * 1024:(r + 1) * 1024, lo_: lo_ + 512]
                            .rearrange("(c p) k -> p c k", p=128))
                    dma(SY, "s2",
                        skb[:, so_:so_ + SLOT].rearrange("p (c k) -> p c k", c=8)[:, :, hl * 512:(hl + 1) * 512],
                        sk_full[gh_].ap()[r * 1024:(r + 1) * 1024, lo_: lo_ + 512]
                            .rearrange("(c p) k -> p c k", p=128))
                    dma(SY, "s3",
                        vbb[:, so_:so_ + SLOT].rearrange("p (c e) -> p c e", c=8)[:, hl * 4:(hl + 1) * 4, :],
                        v_full[gh_].ap()[r * (SH // 2) + lo_: r * (SH // 2) + lo_ + 512, :]
                            .rearrange("(c p) e -> p c e", p=128))
                sup_marks.append((cnt["sl"], cnt["s2"], cnt["s3"]))

            wait(SY, "mm")
            wait(SY, "vv")
            wait(SY, "ac")
            load_super(0)
            load_super(1)
            exp_acv = []
            gchunk = 0
            evac_vv = [0, 0, 0, 0]
            for sp in range(NSUP if "attn" not in dbg else 0):
                for qt in range(NQT):
                    for j in range(9):
                        if j < 8:
                            mk = sup_marks[sp]
                            T.wait_ge(sem["sl"], mk[0])
                            T.wait_ge(sem["s2"], mk[1])
                            T.wait_ge(sem["s3"], mk[2])
                            ps = PS["sc"][gchunk % 3]
                            if gchunk >= 3:
                                T.wait_ge(sem["ac"], exp_acv[gchunk - 3])
                            first = True
                            so_ = (sp % 2) * SLOT
                            for opa, opb in ((ckb, cosq), (skb, sinq)):
                                for dc in range(8):
                                    ins_ = T.matmul(ps[:, 0:256],
                                                    opa[:, so_ + dc * 1024 + j * 128: so_ + dc * 1024 + (j + 1) * 128],
                                                    opb[:, dc * 1024 + qt * 256: dc * 1024 + qt * 256 + 256],
                                                    start=first, stop=(opa is skb and dc == 7))
                                    first = False
                            mmv = inc(ins_, "mm")
                            A.wait_ge(sem["mm"], mmv)
                            acv = inc(A.activation(pts[:, j * 256:(j + 1) * 256], ps[:, 0:256],
                                                   AF.Exp, scale=1.0 / 32.0), "ac")
                            exp_acv.append(acv)
                            gchunk += 1
                        if j > 0:
                            jj = j - 1
                            T.wait_ge(sem["ac"], exp_acv[gchunk - (2 if j < 8 else 1)])
                            for qs in range(2):
                                for eh in range(2):
                                    if jj == 0:
                                        T.wait_ge(sem["vv"], evac_vv[qs * 2 + eh])
                                    T.matmul(pv[qs * 2 + eh][:, 0:512],
                                             pts[:, jj * 256 + qs * 128: jj * 256 + qs * 128 + 128],
                                             vbb[:, (sp % 2) * SLOT + jj * 1024 + eh * 512:
                                                 (sp % 2) * SLOT + jj * 1024 + (eh + 1) * 512],
                                             start=(jj == 0), stop=(jj == 7))
                            ins_ = T.matmul(sm[0:4, 0:256], eoh[:, qt * 4:(qt + 1) * 4],
                                            pts[:, jj * 256:(jj + 1) * 256],
                                            start=(sp == 0 and qt == 0 and jj == 0),
                                            stop=(sp == NSUP - 1 and qt == NQT - 1 and jj == 7))
                    mmv = inc(ins_, "mm")
                    if qt == NQT - 1:
                        cons_marks.append(mmv)
                        if sp + 2 < NSUP:
                            load_super(sp + 2)
                    V.wait_ge(sem["mm"], mmv)
                    for qs in range(2):
                        for eh in range(2):
                            dst = oacc[:, (qt * 2 + qs) * 1024 + eh * 512:
                                       (qt * 2 + qs) * 1024 + (eh + 1) * 512]
                            if sp == 0:
                                vop(V.tensor_copy(dst, pv[qs * 2 + eh][:, 0:512]))
                            else:
                                vop(V.tensor_tensor(out=dst, in0=dst, in1=pv[qs * 2 + eh][:, 0:512], op=ALU.add))
                            evac_vv[qs * 2 + eh] = cnt["vv"]

            wait(V, "mm")
            vop(V.tensor_copy(sums_sb[0:4, :], sm[0:4, 0:256]))
            wait(T, "vv")
            ins_ = T.nop()
            for qs_ in range(2 if "sumt" not in dbg else 0):
                ins_ = T.matmul(pv[0][0:128, qs_ * 4:(qs_ + 1) * 4],
                                sums_sb[0:4, qs_ * 128:(qs_ + 1) * 128],
                                idf[0:4, 0:4], start=(qs_ == 0), stop=(qs_ == 1), is_transpose=True)
            mmv = inc(ins_, "mm")
            V.wait_ge(sem["mm"], mmv)
            vop(V.reciprocal(rinv[:], pv[0][0:128, 0:8]))
            wait(T, "vv")
            es_b.close()

            es_c = ExitStack()
            alloc_psum_build(es_c, f"_c{h}")
            pt_ac = [0, 0]
            x1v = [cnt["vv"], cnt["vv"]]
            sbc = lambda n, sh, dt: es_c.enter_context(nc.sbuf_tensor(n + f"_h{h}", sh, dt))
            x1a = sbc("x1a", [128, 8 * 1024], F32)
            lnx2 = sbc("lnx2", [128, D], BF16)
            lnx2F = sbc("lnx2F", [128, 8 * 128], BF16)
            outt = [sbc("outt0", [128, D], F32), sbc("outt1", [128, D], F32)]

            for sub in range(8):
                buf = xt[sub % 2]
                psem = "xa" if sub % 2 == 0 else "xb"
                SY.wait_ge(sem["vv"], x1v[sub % 2])
                dma(SY, psem, buf[:], x_h[hq0 + sub * 128: hq0 + (sub + 1) * 128, :])
                wait(V, psem)
                rcol = (sub % 2) * 4 + (sub // 2)
                vop(V.scalar_tensor_tensor(out=x1a[:, sub * 1024:(sub + 1) * 1024],
                                           in0=oacc[:, sub * 1024:(sub + 1) * 1024],
                                           scalar=rinv[:, rcol:rcol + 1], in1=buf[:],
                                           op0=ALU.mult, op1=ALU.add))
                x1v[sub % 2] = cnt["vv"]
                vop(V.bn_stats(bnb8[:, sub * 12: sub * 12 + 6], x1a[:, sub * 1024: sub * 1024 + 512]))
                vop(V.bn_stats(bnb8[:, sub * 12 + 6: sub * 12 + 12], x1a[:, sub * 1024 + 512: (sub + 1) * 1024]))
                vop(V.bn_aggr(bna8[:, sub * 2: sub * 2 + 2], bnb8[:, sub * 12: sub * 12 + 12]))
            vop(V.tensor_scalar(out=v8[:], in0=bna8[:, 1::2], scalar1=1.0, scalar2=1e-5,
                                op0=ALU.mult, op1=ALU.add))
            vop(V.reciprocal(v8[:], v8[:]))
            wait(A, "vv")
            acv = inc(A.activation(isd8[:], v8[:], AF.Sqrt), "ac")
            V.wait_ge(sem["ac"], acv)

            for sub in range(8):
                rows = hq0 + sub * 128
                out_b = outt[sub % 2]
                osem = "so" if sub % 2 == 0 else "s4"
                x1s = x1a[:, sub * 1024:(sub + 1) * 1024]
                ln2ap = lnx2[:, :].rearrange("p (ih gh gl d) -> p ih gh gl d", ih=2, gh=4, gl=4, d=32)
                xin = x1s.rearrange("p (ih d gh gl) -> p ih gh gl d", ih=2, d=32, gh=4, gl=4)
                wait(V, "mm")
                vop(V.tensor_scalar(out=ln2ap, in0=xin, scalar1=bna8[:, sub * 2: sub * 2 + 1],
                                    scalar2=isd8[:, sub: sub + 1], op0=ALU.subtract, op1=ALU.mult))
                wait(T, "vv")
                T.wait_ge(sem["ac"], max(pt_ac))
                for ih in range(2):
                    pTb = PS["pT"][ih]
                    for C in range(4):
                        ins_ = T.matmul(pTb[:, C * 128:(C + 1) * 128],
                                        lnx2[:, (ih * 4 + C) * 128:(ih * 4 + C + 1) * 128],
                                        idn[:], start=True, stop=True, is_transpose=True)
                mmv = inc(ins_, "mm")
                A.wait_ge(sem["mm"], mmv)
                for cc_ in range(8):
                    acv = inc(A.activation(lnx2F[:, cc_ * 128:(cc_ + 1) * 128],
                                           PS["pT"][cc_ // 4][:, (cc_ % 4) * 128:(cc_ % 4 + 1) * 128],
                                           AF.Identity), "ac")
                pt_ac[0] = pt_ac[1] = cnt["ac"]
                wait(T, "ac")
                sc = PS["sc"]
                for C in range(4):
                    T.matmul(sc[0][:, 0:512], lnx2F[:, C * 128:(C + 1) * 128],
                             wrT[:, C * 512:(C + 1) * 512], start=(C == 0), stop=False)
                for C in range(4):
                    T.matmul(sc[0][:, 0:512], lnx2F[:, (4 + C) * 128:(4 + C + 1) * 128],
                             nwiT[:, C * 512:(C + 1) * 512], start=False, stop=False)
                T.matmul(sc[0][:, 0:512], onesr[0:1, 0:128], fbr[0:1, 0:512],
                         start=False, stop=True)
                for C in range(4):
                    T.matmul(sc[1][:, 0:512], lnx2F[:, C * 128:(C + 1) * 128],
                             wiT[:, C * 512:(C + 1) * 512], start=(C == 0), stop=False)
                for C in range(4):
                    T.matmul(sc[1][:, 0:512], lnx2F[:, (4 + C) * 128:(4 + C + 1) * 128],
                             wrT[:, C * 512:(C + 1) * 512], start=False, stop=False)
                mmv = inc(T.matmul(sc[1][:, 0:512], onesr[0:1, 0:128], fbr[0:1, 512:1024],
                                   start=False, stop=True), "mm")
                V.wait_ge(sem["mm"], mmv)
                wait(V, osem)
                vop(V.tensor_tensor(out=out_b[:, 0:512], in0=x1s[:, 0:512], in1=sc[0][:, 0:512], op=ALU.add))
                vop(V.tensor_tensor(out=out_b[:, 512:1024], in0=x1s[:, 512:1024], in1=sc[1][:, 0:512], op=ALU.add))
                wait(T, "vv")
                wait(SY, "vv")
                dma(SY, osem, y_h[rows:rows + 128, :], out_b[:])
            wait(V, "so")
            wait(V, "s4")
            wait(T, "so")
            wait(T, "s4")
            es_c.close()

        wait(SY, "so")
        wait(SY, "s4")
    return nc


_NC_CACHE = {}


def kernel(L0_attn_x, L0_attn_Wq, L0_attn_bq, L0_attn_Wk, L0_attn_bk, L0_attn_Wv,
           L0_attn_bv, L0_attn_phase_bias, L0_ffn_G1_R, L0_ffn_G2_R, L0_ffn_G1_I,
           L0_ffn_G2_I, L0_ffn_bias_real, L0_ffn_bias_imag):
    x = np.ascontiguousarray(np.asarray(L0_attn_x, dtype=np.float32).reshape(S, D))
    wq = np.ascontiguousarray(np.asarray(L0_attn_Wq, dtype=np.float32))
    wk = np.ascontiguousarray(np.asarray(L0_attn_Wk, dtype=np.float32))
    wv = np.ascontiguousarray(np.asarray(L0_attn_Wv, dtype=np.float32))
    bq2 = np.ascontiguousarray(np.asarray(L0_attn_bq, dtype=np.float32).reshape(8, 128).T)
    bk2 = np.ascontiguousarray(np.asarray(L0_attn_bk, dtype=np.float32).reshape(8, 128).T)
    pb2 = np.ascontiguousarray(np.asarray(L0_attn_phase_bias, dtype=np.float32).reshape(8, 128).T)
    bvrow = np.ascontiguousarray(np.asarray(L0_attn_bv, dtype=np.float32).reshape(1, D))
    wl = np.arange(1, D + 1, dtype=np.float64) * (2.0 * np.pi / D)
    iwl2 = np.ascontiguousarray((1.0 / (wl + 1e-8)).astype(np.float32).reshape(8, 128).T)
    g1r = np.ascontiguousarray(np.asarray(L0_ffn_G1_R, dtype=np.float32).reshape(32, 4, 32)
                               .transpose(1, 0, 2).reshape(4, 1024))
    g1i = np.ascontiguousarray(np.asarray(L0_ffn_G1_I, dtype=np.float32).reshape(32, 4, 32)
                               .transpose(1, 0, 2).reshape(4, 1024))
    g2r = np.ascontiguousarray(np.asarray(L0_ffn_G2_R, dtype=np.float32).reshape(4, 256))
    g2i = np.ascontiguousarray(np.asarray(L0_ffn_G2_I, dtype=np.float32).reshape(4, 256))
    fbrow = np.ascontiguousarray(
        np.concatenate([np.asarray(L0_ffn_bias_real, dtype=np.float32),
                        np.asarray(L0_ffn_bias_imag, dtype=np.float32)]).reshape(1, D))
    idn = np.eye(128, dtype=np.float32).astype(BF)
    idf = np.eye(8, dtype=np.float32)
    eoh = np.zeros((128, 16), dtype=np.float32)
    for t in range(4):
        eoh[:, 4 * t + t] = 1.0
    eoh = eoh.astype(BF)
    onesrow = np.ones((1, 128), dtype=np.float32).astype(BF)

    if "nc" not in _NC_CACHE:
        _NC_CACHE["nc"] = build_nc()
    nc = _NC_CACHE["nc"]

    shared = dict(wq=wq, wk=wk, wv=wv, bq2=bq2, bk2=bk2, pb2=pb2, bvrow=bvrow,
                  iwl=iwl2, g1r=g1r, g1i=g1i, g2r=g2r, g2i=g2i, fbrow=fbrow,
                  idn=idn, idf=idf, eoh=eoh, onesrow=onesrow)
    in_maps = [dict(shared, x_own=np.ascontiguousarray(x[c * SH:(c + 1) * SH]))
               for c in range(NCORE)]
    res = run_bass_kernel_spmd(nc, in_maps, core_ids=list(range(NCORE)))
    y = np.concatenate([res.results[c]["y_own"] for c in range(NCORE)], axis=0)
    return np.ascontiguousarray(y.reshape(1, S, D))
